# version 25
# speedup vs baseline: 1.1819x; 1.0089x over previous
import os
import numpy as np
import concourse.bass as bass
import concourse.mybir as mybir
from concourse.bass_utils import run_bass_kernel_spmd
from concourse.bass_core_rust import IndirectOffsetOnAxis

F32 = mybir.dt.float32
BF16 = mybir.dt.bfloat16
I32 = mybir.dt.int32
AF = mybir.ActivationFunctionType
ALU = mybir.AluOpType
AX = mybir.AxisListType.X

B, L, D = 16, 2048, 1024
NCORES = 8
BPC = B // NCORES
TOK = BPC * L
NT = TOK // 128
H, DH, WLEN = 8, 64, 64
NQ, NKV = 512, 1024
LN_EPS = 1e-5
SC = DH ** -0.5
NITERS = 50
LOGK_Q = float(np.log(np.float32(NQ * 9.0 / 8.0)))
LOGK_KV = float(np.log(np.float32(NKV * 9.0 / 8.0)))
SENT = 1 << 22
_LAST_SCHED = None

SAFE_SYNC = True

ENGINES = ("pe", "act", "dve", "pool", "sp")
_ENG_ATTR = dict(pe="tensor", act="scalar", dve="vector", pool="gpsimd", sp="sync")


class Op:
    __slots__ = ("eng", "fn", "reads", "writes", "deps", "need_inc", "idx",
                 "is_dma", "dma_ch", "name", "waits", "self_wait")

    def __init__(self, eng, fn, reads, writes, name, is_dma=False, dma_ch=None):
        self.eng = eng
        self.fn = fn
        self.reads = reads
        self.writes = writes
        self.name = name
        self.is_dma = is_dma
        self.dma_ch = dma_ch
        self.deps = []
        self.need_inc = False
        self.waits = []
        self.self_wait = None


class Sched:

    def __init__(self, nc):
        self.nc = nc
        self.ops = []
        self.last_w = {}
        self.readers = {}
        self.cur_fence = None

    def set_fence(self, key):
        self.cur_fence = key

    def op(self, eng, fn, reads=(), writes=(), name="", is_dma=False, dma_ch=None):
        o = Op(eng, fn, list(reads), list(writes), name, is_dma, dma_ch)
        if (self.cur_fence is not None and self.cur_fence not in o.writes
                and self.cur_fence not in o.reads
                and any(k not in self.last_w for k in o.writes)):
            o.reads.append(self.cur_fence)
        idx = len(self.ops)
        o.idx = idx
        deps = set()
        for k in o.reads:
            if k in self.last_w:
                deps.add(self.last_w[k])
        for k in o.writes:
            if k in self.last_w:
                deps.add(self.last_w[k])
            for r in self.readers.get(k, ()):
                deps.add(r)
        for k in o.reads:
            self.readers.setdefault(k, []).append(idx)
        for k in o.writes:
            self.last_w[k] = idx
            self.readers[k] = []
        deps.discard(idx)
        o.deps = sorted(deps)
        self.ops.append(o)
        return idx

    def dma(self, ch, eng, out, in_, reads=(), writes=(), name=""):
        def fn(e, _out=out, _in=in_):
            return e.dma_start(_out, _in)
        return self.op(eng, fn, reads, writes, name, is_dma=True, dma_ch=ch)

    def finalize_and_emit(self, es):
        nc = self.nc
        n = len(self.ops)
        for o in self.ops:
            for d in o.deps:
                do = self.ops[d]
                if do.is_dma:
                    do.need_inc = True
                    continue
                if do.eng == o.eng and not o.is_dma:
                    if (not SAFE_SYNC) or do.eng == "pe":
                        continue
                do.need_inc = True

        eng_sem = {e: es.enter_context(nc.semaphore(f"S_{e}")) for e in ENGINES}
        eng_cnt = {e: 0 for e in ENGINES}
        op_semval = [None] * n
        for o in self.ops:
            if o.is_dma:
                op_semval[o.idx] = o.dma_ch.assign(self, es)
            elif o.need_inc:
                eng_cnt[o.eng] += 1
                op_semval[o.idx] = (eng_sem[o.eng], eng_cnt[o.eng])
        self._op_semval = op_semval

        for o in self.ops:
            need = {}
            for d in o.deps:
                do = self.ops[d]
                if (not do.is_dma) and do.eng == o.eng and not o.is_dma:
                    if (not SAFE_SYNC) or do.eng == "pe":
                        continue
                sv = op_semval[d]
                if sv is None:
                    continue
                s, v = sv
                if need.get(id(s), (None, -1))[1] < v:
                    need[id(s)] = (s, v)
            o.waits = list(need.values())
            if o.is_dma:
                w = o.dma_ch.self_wait(o.idx, self)
                if w is not None:
                    o.self_wait = w

        with nc.Block() as block:
            for e in ENGINES:
                ops_e = [o for o in self.ops if o.eng == e]
                if not ops_e:
                    continue
                sect = getattr(block, _ENG_ATTR[e])

                def make(section_ops):
                    def emit(eng):
                        waited = {}
                        for o in section_ops:
                            cand = {}
                            allw = o.waits + ([o.self_wait] if o.self_wait else [])
                            for s2, v2 in allw:
                                if v2 > cand.get(id(s2), (None, -1))[1]:
                                    cand[id(s2)] = (s2, v2)
                            ws = [sv for sv in cand.values()
                                  if sv[1] > waited.get(id(sv[0]), -1)]
                            ws.sort(key=lambda sv: -sv[1])
                            for s, v in ws:
                                waited[id(s)] = max(waited.get(id(s), -1), v)
                            for s, v in ws[1:]:
                                eng.wait_ge(s, v)
                            inst = o.fn(eng)
                            if ws:
                                inst._wait_ge(ws[0][0], ws[0][1])
                            sv = self._op_semval[o.idx]
                            if sv is not None:
                                inst.then_inc(sv[0], 16 if o.is_dma else 1)
                    return emit
                sect(make(ops_e))



class DmaChan:

    def __init__(self, name, nsem=4):
        self.name = name
        self.nsem = nsem
        self.sems = None
        self.counts = None
        self.seq = 0

    def assign(self, sched, es):
        if self.sems is None:
            self.sems = [es.enter_context(sched.nc.semaphore(f"D_{self.name}{i}"))
                         for i in range(self.nsem)]
            self.counts = [0] * self.nsem
        slot = self.seq % self.nsem
        self.counts[slot] += 16
        self.seq += 1
        return self.sems[slot], self.counts[slot]

    def self_wait(self, op_idx, sched):
        s, v = sched._op_semval[op_idx]
        if v > 16:
            return (s, v - 16)
        return None


def _r(x):
    return np.ascontiguousarray(x)




_PH = set(os.environ.get("KPH", "light,route,gather,heavy,tail").split(","))
_BUD = {"pe": 999, "sp": 999, "dve": 4, "act": 2, "pool": 1}
for _kv in os.environ.get("KBUD", "").split(","):
    if "=" in _kv:
        _k, _v = _kv.split("=")
        _BUD[_k] = int(_v)


def build_program(debug=()):
    nc = bass.Bass("TRN2", target_bir_lowering=False, debug=False)
    from contextlib import ExitStack
    es = ExitStack()
    sch = Sched(nc)
    global _LAST_SCHED
    _LAST_SCHED = sch
    dbg_names = set(debug)

    x_d = nc.dram_tensor("x", [TOK, D], F32, kind="ExternalInput").ap()
    y_d = nc.dram_tensor("y", [TOK, D], F32, kind="ExternalOutput").ap()
    wd = {}
    for name, shape, dt in [
        ("W_in", [D, 2 * D], F32), ("WqkvT", [D, 3 * H * DH], F32),
        ("WqT", [D, H * DH], F32), ("WkvT", [D, 2 * H * DH], F32),
        ("WoutlT", [H * DH, D], F32), ("WouthT", [H * DH, D], F32),
        ("W_out", [D, D], F32),
        ("gamma_r", [128, 8], F32), ("beta_r", [128, 8], F32),
        ("A_r", [128, 8], F32), ("b_in_r", [128, 16], F32),
        ("b_out_r", [128, 8], F32), ("ln_w_r", [128, 8], F32),
        ("ln_b_r", [128, 8], F32), ("gamh_r", [128, 8], F32),
        ("rt2_r", [128, 16], F32),
        ("nullk_row", [1, H * DH], F32), ("nullv_row", [1, H * DH], F32),
        ("nullq_row", [1, D], F32),
        ("ident_bf", [128, 128], BF16), ("ident_f32", [128, 128], F32),
        ("cimm", [128, 16], F32),
        ("asg4", [4, 128], F32), ("asg4T", [128, 4], F32),
        ("ltri", [128, 128], F32), ("logk4", [4, 1], F32),
        ("kpp", [128, 1], F32), ("rbase", [128, 1], F32),
        ("nullk_col", [128, 4], F32), ("nullv65", [1, 8 * 65], F32),
        ("bout_row", [1, D], F32),
    ]:
        wd[name] = nc.dram_tensor(name, shape, dt, kind="ExternalInput").ap()

    def scratch(name, shape, dt):
        kind = "ExternalOutput" if name in dbg_names else "Internal"
        return nc.dram_tensor(name, shape, dt, kind=kind).ap()

    xn2T_dr = scratch("xn2T_dr", [128, 8 * TOK], BF16)
    xnr_dr = scratch("xnr_dr", [TOK, D], BF16)
    U_dr = scratch("U_dr", [128, 8 * TOK], F32)
    G_dr = scratch("G_dr", [128, 8 * TOK], F32)
    rec_dr = scratch("rec_dr", [TOK, D], BF16)
    s_dr = scratch("s_dr", [2, TOK], F32)
    avlT_dr = scratch("avlT_dr", [128, 4 * TOK], BF16)
    xq_dr = scratch("xq_dr", [BPC * NQ, D + 4], BF16)
    ffnh_dr = scratch("ffnh_dr", [TOK, D], BF16)
    qi_dr = scratch("qi_dr", [BPC, L], I32)
    kvb_dr = scratch("kvb_dr", [BPC, L], F32)

    _cms = {}

    _sb_order = []

    def sb(name, shape, dt, side=None):
        cm = nc.sbuf_tensor(name, shape, dt, side=side)
        t = cm.__enter__()
        _cms[name] = cm
        if side == "right":
            _sb_order.append(name)
        return t

    def sb_free_back_to(marker):
        while _sb_order and _sb_order[-1] != marker:
            n = _sb_order.pop()
            _cms.pop(n).__exit__(None, None, None)

    def sb_free(*names):
        for n in names:
            _cms.pop(n).__exit__(None, None, None)
            if n in _sb_order:
                _sb_order.remove(n)

    psum = lambda name, shape, dt=F32: es.enter_context(nc.psum_tensor(name, shape, dt))

    stats = sb("stats", [128, 2 * NT], F32)
    consts = sb("consts", [128, 96], F32)
    cim = sb("cim", [128, 16], F32)
    ident_b = sb("ident_b", [128, 128], BF16)
    ident_f = sb("ident_f", [128, 128], F32)

    c_gamma = consts[:, 0:8]
    c_beta = consts[:, 8:16]
    c_A = consts[:, 16:24]
    c_bout = consts[:, 24:32]
    c_lnw = consts[:, 32:40]
    c_lnb = consts[:, 40:48]
    c_gamh = consts[:, 48:56]
    c_bin = consts[:, 56:72]
    c_rt2 = consts[:, 72:88]
    c_mA = consts[:, 88:96]

    CI_EPS, CI_NH, CI_H, CI_ONE = 0, 1, 2, 6

    chL = DmaChan("L", 8)
    chS = DmaChan("S", 8)
    chW = DmaChan("W", 4)
    chU = DmaChan("U", 8)
    chI = DmaChan("I", 4)

    sch.dma(chL, "sp", cim[:], wd["cimm"], writes=["cimm"], name="ld_cimm")
    sch.dma(chL, "sp", ident_b[:], wd["ident_bf"], writes=["identb"], name="ld_idb")
    sch.dma(chL, "sp", ident_f[:], wd["ident_f32"], writes=["identf"], name="ld_idf")
    col = 0
    for nm, w in [("gamma_r", 8), ("beta_r", 8), ("A_r", 8), ("b_out_r", 8),
                  ("ln_w_r", 8), ("ln_b_r", 8), ("gamh_r", 8), ("b_in_r", 16),
                  ("rt2_r", 16)]:
        sch.dma(chL, "sp", consts[:, col:col + w], wd[nm], writes=["consts"], name=f"ld_{nm}")
        col += w
    sch.op("dve", lambda e: e.tensor_scalar(c_mA, c_A, -1.0, None, ALU.mult),
           reads=["consts"], writes=["consts"], name="negA")

    xtile = [sb(f"xt{i}", [128, D], F32, side="right") for i in range(2)]
    xn_f = [sb(f"xn{i}", [128, D], F32, side="right") for i in range(2)]
    xn2b = [sb(f"x2b{i}", [128, D], BF16, side="right") for i in range(2)]
    xnrb = [sb(f"xnrb{i}", [128, D], BF16, side="right") for i in range(2)]
    x2Ts = [sb(f"x2Ts{i}", [128, D], BF16, side="right") for i in range(2)]
    scrA = sb("scrA", [128, D], F32, side="right")
    xnT_r = [sb(f"xnT{i}", [128, 8 * 128], F32, side="right") for i in range(2)]
    sm = sb("sm", [128, 16], F32)
    s_stage = sb("s_stage", [2, 128], F32)
    ps_acc = [psum(f"ps_acc{i}", [128, 512], F32) for i in range(4)]
    ps_bT = psum("ps_bT", [128, 1024], BF16)
    ps_f = [psum(f"ps_f{i}", [128, 512], F32) for i in range(2)]
    ps_misc = psum("ps_misc", [128, 512], F32)
    ps_t = ps_f
    ps_b = ps_bT
    ps_s = ps_misc

    RCP = 1.0 / D

    pend_low = []
    win_emitted = [False, False, False, False]

    def emit_p0_tile(j, sl, EM):
        kx, kxn, kx2, kxr, kts = f"xt{sl}", f"xn{sl}", f"x2b{sl}", f"xnrb{sl}", f"x2Ts{sl}"
        EM("sp", lambda: sch.dma(chL, "sp", xtile[sl][:], x_d[j * 128:(j + 1) * 128, :],
                writes=[kx], name=f"ldx{j}"))
        EM("dve", lambda: sch.op("dve", lambda e, t=xtile[sl]: e.tensor_reduce(sm[:, 0:1], t[:], AX, ALU.add, negate=True),
               reads=[kx], writes=["sm0"], name=f"sumx{j}"))
        EM("act", lambda: sch.op("act", lambda e, t=xtile[sl]: e.activation(scrA[:, 0:512], t[:, 0:512], AF.Square, accum_out=sm[:, 4:5]),
               reads=[kx], writes=["scrA", "ss1"], name=f"sqxa{j}"))
        EM("act", lambda: sch.op("act", lambda e, t=xtile[sl]: e.activation(scrA[:, 512:1024], t[:, 512:1024], AF.Square, accum_out=sm[:, 2:3]),
               reads=[kx], writes=["scrA", "mu2"], name=f"sqxb{j}"))
        EM("dve", lambda: sch.op("dve", lambda e: e.tensor_tensor(sm[:, 4:5], sm[:, 4:5], sm[:, 2:3], ALU.add),
               reads=["ss1", "mu2"], writes=["ss1"], name=f"sqxc{j}"))
        EM("dve", lambda: sch.op("dve", lambda e: e.tensor_scalar(sm[:, 1:2], sm[:, 0:1], RCP, None, ALU.mult),
               reads=["sm0"], writes=["negmu"], name=f"mu{j}"))
        EM("dve", lambda: sch.op("dve", lambda e: e.tensor_tensor(sm[:, 2:3], sm[:, 1:2], sm[:, 1:2], ALU.mult),
               reads=["negmu"], writes=["mu2"], name=f"mu2{j}"))
        EM("dve", lambda: sch.op("dve", lambda e: e.scalar_tensor_tensor(sm[:, 3:4], sm[:, 4:5], RCP, sm[:, 2:3], ALU.mult, ALU.subtract),
               reads=["ss1", "mu2"], writes=["var1"], name=f"var{j}"))
        EM("dve", lambda: sch.op("dve", lambda e: e.tensor_scalar(
            sm[:, 5:6], sm[:, 3:4], LN_EPS, None, ALU.add),
            reads=["var1"], writes=["lnv"], name=f"rsA0{j}"))
        EM("dve", lambda: sch.op("dve", lambda e: e.tensor_scalar(
            sm[:, 0:1].bitcast(I32), sm[:, 5:6].bitcast(I32), 1, None, ALU.arith_shift_right),
            reads=["lnv"], writes=["sm0"], name=f"rsA1{j}"))
        EM("dve", lambda: sch.op("dve", lambda e: e.tensor_scalar(
            sm[:, 0:1].bitcast(I32), sm[:, 0:1].bitcast(I32), -1, 0x5f3759df, ALU.mult, ALU.add),
            reads=["sm0"], writes=["sm0"], name=f"rsA2{j}"))
        for it3 in range(3):
            EM("dve", lambda it3=it3: sch.op("dve", lambda e: e.tensor_tensor(
                sm[:, 2:3], sm[:, 0:1], sm[:, 0:1], ALU.mult),
                reads=["sm0"], writes=["mu2"], name=f"rsA3{j}_{it3}"))
            EM("dve", lambda it3=it3: sch.op("dve", lambda e: e.tensor_tensor(
                sm[:, 2:3], sm[:, 2:3], sm[:, 5:6], ALU.mult),
                reads=["mu2", "lnv"], writes=["mu2"], name=f"rsA4{j}_{it3}"))
            EM("dve", lambda it3=it3: sch.op("dve", lambda e: e.tensor_scalar(
                sm[:, 2:3], sm[:, 2:3], -0.5, 1.5, ALU.mult, ALU.add),
                reads=["mu2"], writes=["mu2"], name=f"rsA5{j}_{it3}"))
            EM("dve", lambda it3=it3: sch.op("dve", lambda e: e.tensor_tensor(
                sm[:, 0:1], sm[:, 0:1], sm[:, 2:3], ALU.mult),
                reads=["sm0", "mu2"], writes=["sm0"], name=f"rsA6{j}_{it3}"))
        EM("dve", lambda: sch.op("dve", lambda e, jj=j: e.tensor_copy(
            stats[:, NT + jj:NT + jj + 1], sm[:, 0:1]),
            reads=["sm0"], writes=[f"r1.{j}"], name=f"r1_{j}"))
        EM("dve", lambda: sch.op("dve", lambda e, jj=j: e.tensor_tensor(stats[:, jj:jj + 1], sm[:, 1:2], stats[:, NT + jj:NT + jj + 1], ALU.mult),
               reads=["negmu", f"r1.{j}"], writes=[f"nb1.{j}"], name=f"nb1_{j}"))
        EM("dve", lambda: sch.op("dve", lambda e, t=xtile[sl], o=xn_f[sl], jj=j: e.tensor_scalar(
            o[:], t[:], stats[:, NT + jj:NT + jj + 1], stats[:, jj:jj + 1], ALU.mult, ALU.add),
            reads=[kx, f"r1.{j}", f"nb1.{j}"], writes=[kxn], name=f"xn{j}"))
        EM("dve", lambda: sch.op("dve", lambda e, t=xn_f[sl]: e.tensor_reduce(sm[:, 6:7], t[:], AX, ALU.add, negate=True),
               reads=[kxn], writes=["sm6"], name=f"sumxn{j}"))
        EM("act", lambda: sch.op("act", lambda e, t=xn_f[sl]: e.activation(scrA[:, 0:512], t[:, 0:512], AF.Square, accum_out=sm[:, 7:8]),
               reads=[kxn], writes=["scrA", "ss2"], name=f"sqxna{j}"))
        EM("act", lambda: sch.op("act", lambda e, t=xn_f[sl]: e.activation(scrA[:, 512:1024], t[:, 512:1024], AF.Square, accum_out=sm[:, 9:10]),
               reads=[kxn], writes=["scrA", "mu22"], name=f"sqxnb{j}"))
        EM("dve", lambda: sch.op("dve", lambda e: e.tensor_tensor(sm[:, 7:8], sm[:, 7:8], sm[:, 9:10], ALU.add),
               reads=["ss2", "mu22"], writes=["ss2"], name=f"sqxnc{j}"))
        EM("dve", lambda: sch.op("dve", lambda e: e.tensor_scalar(sm[:, 8:9], sm[:, 6:7], RCP, None, ALU.mult),
               reads=["sm6"], writes=["negmu2"], name=f"mu_2{j}"))
        EM("dve", lambda: sch.op("dve", lambda e: e.tensor_tensor(sm[:, 9:10], sm[:, 8:9], sm[:, 8:9], ALU.mult),
               reads=["negmu2"], writes=["mu22"], name=f"mu22{j}"))
        EM("dve", lambda: sch.op("dve", lambda e: e.scalar_tensor_tensor(sm[:, 10:11], sm[:, 7:8], RCP, sm[:, 9:10], ALU.mult, ALU.subtract),
               reads=["ss2", "mu22"], writes=["var2"], name=f"var2{j}"))
        EM("dve", lambda: sch.op("dve", lambda e: e.tensor_scalar(
            sm[:, 11:12], sm[:, 10:11], LN_EPS, None, ALU.add),
            reads=["var2"], writes=["lnv2"], name=f"rsB0{j}"))
        EM("dve", lambda: sch.op("dve", lambda e: e.tensor_copy(sm[:, 12:13], sm[:, 7:8]),
            reads=["ss2"], writes=["lnv2"], name=f"rsB1{j}"))
        EM("dve", lambda: sch.op("dve", lambda e: e.tensor_scalar(
            sm[:, 13:15].bitcast(I32), sm[:, 11:13].bitcast(I32), 1, None, ALU.arith_shift_right),
            reads=["lnv2"], writes=["lnss", "nb2"], name=f"rsB2{j}"))
        EM("dve", lambda: sch.op("dve", lambda e: e.tensor_scalar(
            sm[:, 13:15].bitcast(I32), sm[:, 13:15].bitcast(I32), -1, 0x5f3759df, ALU.mult, ALU.add),
            reads=["lnss"], writes=["lnss"], name=f"rsB3{j}"))
        for it3 in range(3):
            EM("dve", lambda it3=it3: sch.op("dve", lambda e: e.tensor_tensor(
                sm[:, 5:7], sm[:, 13:15], sm[:, 13:15], ALU.mult),
                reads=["lnss"], writes=["lnv", "sm6"], name=f"rsB4{j}_{it3}"))
            EM("dve", lambda it3=it3: sch.op("dve", lambda e: e.tensor_tensor(
                sm[:, 5:7], sm[:, 5:7], sm[:, 11:13], ALU.mult),
                reads=["lnv", "lnv2"], writes=["lnv"], name=f"rsB5{j}_{it3}"))
            EM("dve", lambda it3=it3: sch.op("dve", lambda e: e.tensor_scalar(
                sm[:, 5:7], sm[:, 5:7], -0.5, 1.5, ALU.mult, ALU.add),
                reads=["lnv"], writes=["lnv"], name=f"rsB6{j}_{it3}"))
            EM("dve", lambda it3=it3: sch.op("dve", lambda e: e.tensor_tensor(
                sm[:, 13:15], sm[:, 13:15], sm[:, 5:7], ALU.mult),
                reads=["lnss", "lnv"], writes=["lnss"], name=f"rsB7{j}_{it3}"))
        EM("dve", lambda: sch.op("dve", lambda e: e.tensor_copy(sm[:, 12:13], sm[:, 13:14]),
            reads=["lnss"], writes=["r2"], name=f"r2_{j}"))
        EM("dve", lambda: sch.op("dve", lambda e: e.tensor_tensor(sm[:, 13:14], sm[:, 8:9], sm[:, 12:13], ALU.mult),
               reads=["negmu2", "r2", "lnss"], writes=["nb2", "lnss"], name=f"nb2_{j}"))
        EM("dve", lambda: sch.op("dve", lambda e: e.tensor_scalar(
            sm[:, 15:16], sm[:, 14:15], 32.0, 3.2e13, ALU.mult, ALU.min),
            reads=["lnss", "nb2"], writes=["rho"], name=f"rho{j}"))
        EM("dve", lambda: sch.op("dve", lambda e, t=xn_f[sl], o=xnrb[sl]: e.tensor_scalar(
            o[:], t[:], sm[:, 15:16], None, ALU.mult),
            reads=[kxn, "rho"], writes=[kxr], name=f"xnr{j}"))
        EM("sp", lambda: sch.dma(chS, "sp", xnr_dr[j * 128:(j + 1) * 128, :], xnrb[sl][:],
                reads=[kxr], writes=[f"xnrd.{j}"], name=f"stxnr{j}"))
        EM("dve", lambda: sch.op("dve", lambda e, t=xn_f[sl], o=xn2b[sl]: e.tensor_scalar(
            o[:], t[:], sm[:, 12:13], sm[:, 13:14], ALU.mult, ALU.add),
            reads=[kxn, "r2", "nb2"], writes=[kx2], name=f"xn2{j}"))
        def trb():
            for k in range(8):
                sch.op("pe", lambda e, t=xn2b[sl], kk=k: e.transpose(
                    ps_b[:, kk * 128:(kk + 1) * 128], t[:, kk * 128:(kk + 1) * 128], ident_b[:]),
                    reads=[kx2, "identb"], writes=["pk4"], name=f"trb{j}_{k}")
        EM("pe", trb)
        EM("dve", lambda: sch.op("dve", lambda e, o=x2Ts[sl]: e.tensor_copy(o[:], ps_b[:]),
               reads=["pk4"], writes=[kts], name=f"evb{j}"))
        EM("sp", lambda: sch.dma(chS, "sp",
                xn2T_dr.rearrange("p (dt t) -> p dt t", dt=8)[:, :, j * 128:(j + 1) * 128],
                x2Ts[sl][:].rearrange("p (dt t) -> p dt t", dt=8),
                reads=[kts], writes=[f"x2Td.{j}"], name=f"stx2T{j}"))
        def trf():
            for k in range(8):
                sch.op("pe", lambda e, t=xn_f[sl], kk=k: e.transpose(
                    ps_t[kk // 4][:, (kk % 4) * 128:(kk % 4 + 1) * 128], t[:, kk * 128:(kk + 1) * 128], ident_f[:]),
                    reads=[kxn, "identf"], writes=[f"pk{5 + k // 4}"], name=f"trf{j}_{k}")
        EM("pe", trf)
        for half in range(2):
            EM("dve", lambda half=half: sch.op("dve", lambda e, hh=half, ss=sl: e.tensor_copy(
                xnT_r[ss][:, hh * 512:(hh + 1) * 512], ps_t[hh][:]),
                reads=[f"pk{5 + half}"], writes=[f"xnT{sl}.{half}"], name=f"evf{j}_{half}"))
        def smm():
            for k in range(8):
                sch.op("pe", lambda e, kk=k, ss=sl: e.matmul(
                    ps_s[0:2, 0:128], c_rt2[:, kk * 2:(kk + 1) * 2], xnT_r[ss][:, kk * 128:(kk + 1) * 128],
                    start=(kk == 0), stop=(kk == 7)),
                    reads=[f"xnT{sl}.{k // 4}", "consts"], writes=["pk7"], name=f"smm{j}_{k}")
        EM("pe", smm)
        EM("dve", lambda: sch.op("dve", lambda e, jj=j: e.tensor_copy(s_stage[:], ps_s[0:2, 0:128]),
               reads=["pk7"], writes=["s_st"], name=f"sev{j}"))
        EM("sp", lambda: sch.dma(chS, "sp", s_dr[:, j * 128:(j + 1) * 128], s_stage[:],
                reads=["s_st"], writes=[f"s_dr.{j // 4}"], name=f"sst{j}"))


    Wf_in = sb("Wf_in", [128, 8 * 2048], BF16, side="right")
    beta_b = sb("beta_b", [128, 8], BF16)
    bcrow = sb("bcrow", [1, 2048], F32, side="right")
    Bc_pp = sb("Bc_pp", [128, 16], F32)
    stg = [sb(f"stg{i}", [128, 512], F32) for i in range(2)]
    x2c = [sb(f"x2c{i}", [128, 8 * 512], BF16) for i in range(2)]
    ps_w = ps_acc
    ps_c = ps_misc
    ps_r = ps_misc

    def emit_win_prep():
        sch.dma(chW, "pool", Wf_in[:].rearrange("p (dt c) -> p dt c", dt=8),
                wd["W_in"].rearrange("(dt p) c -> p dt c", p=128),
                writes=["Wf_in"], name="ld_Win")
        sch.op("dve", lambda e: e.tensor_copy(beta_b[:], c_beta),
               reads=["consts"], writes=["beta_b"], name="betab")
        for cc in range(4):
            for dt in range(8):
                sch.op("pe", lambda e, c=cc, d=dt: e.matmul(
                    ps_c[0:1, :], beta_b[:, d:d + 1], Wf_in[:, d * 2048 + c * 512:d * 2048 + (c + 1) * 512],
                    start=(d == 0), stop=(d == 7)),
                    reads=["beta_b", "Wf_in"], writes=["pk7"], name=f"bc{cc}_{dt}")
            sch.op("dve", lambda e, c=cc: e.tensor_copy(bcrow[:, c * 512:(c + 1) * 512], ps_c[0:1, :]),
                   reads=["pk7"], writes=[f"bcrow.{cc}"], name=f"bcev{cc}")
        for ct in range(16):
            sch.op("pe", lambda e, c=ct: e.matmul(
                ps_r[:, c:c + 1], bcrow[:, c * 128:(c + 1) * 128], ident_f[0:1, 0:1],
                start=True, stop=True),
                reads=[f"bcrow.{ct // 4}", "identf"], writes=["pk7"], name=f"bctr{ct}")
        sch.op("dve", lambda e: e.tensor_copy(Bc_pp[:], ps_r[:, 0:16]),
               reads=["pk7"], writes=["Bc_pp"], name="bcev2")
        sch.op("dve", lambda e: e.tensor_tensor(Bc_pp[:], Bc_pp[:], c_bin, ALU.add),
               reads=["Bc_pp", "consts"], writes=["Bc_pp"], name="bcadd")
        for dt in range(8):
            for q4 in range(4):
                sch.op("dve", lambda e, d=dt, q4=q4: e.tensor_scalar(
                    Wf_in[:, d * 2048 + q4 * 512:d * 2048 + (q4 + 1) * 512],
                    Wf_in[:, d * 2048 + q4 * 512:d * 2048 + (q4 + 1) * 512],
                    c_gamma[:, d:d + 1], None, ALU.mult),
                    reads=["Wf_in", "consts"], writes=["Wf_in"], name=f"fold{dt}_{q4}")

    WIN_BKS = [2, 3, 5, 6]
    WIN_PS = [ps_acc[2], ps_acc[3], ps_f[0], ps_f[1]]

    def emit_win_chunk(tc, b, sl, EM):
        jlist = [b * 16 + tc * 4 + q for q in range(4)]
        EM("sp", lambda: sch.dma(chU, "sp",
                x2c[sl][:].rearrange("p (dt t) -> p dt t", dt=8),
                xn2T_dr.rearrange("p (dt t) -> p dt t", dt=8)[:, :, (b * 2048 + tc * 512):(b * 2048 + (tc + 1) * 512)],
                reads=[f"x2Td.{j}" for j in jlist], writes=[f"x2c{sl}"], name=f"ldx2c{tc}_{b}"))
        for cg in range(4):
            def mm(cg=cg):
                for q in range(4):
                    ct = cg * 4 + q
                    for dt in range(8):
                        sch.op("pe", lambda e, c=ct, d=dt, s=sl, bank=q: e.matmul(
                            WIN_PS[bank][:],
                            Wf_in[:, d * 2048 + c * 128:d * 2048 + (c + 1) * 128],
                            x2c[s][:, d * 512:(d + 1) * 512],
                            start=(d == 0), stop=(d == 7)),
                            reads=["Wf_in", f"x2c{sl}"], writes=[f"pk{WIN_BKS[q]}"], name=f"ug{tc}{b}_{ct}_{dt}")
            EM("pe", mm)
            for q in range(4):
                ct = cg * 4 + q
                EM("dve", lambda ct=ct, q=q: sch.op("dve", lambda e, c=ct, bank=q, st=q % 2: e.tensor_scalar(
                    stg[st][:], WIN_PS[bank][:], Bc_pp[:, c:c + 1], None, ALU.add),
                    reads=[f"pk{WIN_BKS[q]}", "Bc_pp"], writes=[f"stg{q % 2}"], name=f"ugev{tc}{b}_{ct}"))
                def st_(ct=ct, q=q, tc=tc, b=b):
                    dst = U_dr if ct < 8 else G_dr
                    dg = ct % 8
                    off = dg * TOK + b * 2048 + tc * 512
                    sch.dma(chS, "sp", dst[:, off:off + 512], stg[q % 2][:],
                            reads=[f"stg{q % 2}"], writes=[f"{'Ud' if ct < 8 else 'Gd'}.{dg}.{b}.{tc}"],
                            name=f"ugst{tc}{b}_{ct}")
                EM("sp", st_)

    def DIRECT(tag, fn):
        fn()

    def QUEUE(tag, fn):
        pend_low.append((tag, fn))

    emit_win_prep()
    for tc in range(4):
        EMf = DIRECT if tc == 0 else QUEUE
        for q in range(4):
            for b in range(BPC):
                emit_p0_tile(b * 16 + tc * 4 + q, b, EMf)
        for b in range(BPC):
            emit_win_chunk(tc, b, b, EMf)
        if tc == 0:
            win_emitted[0] = True
        else:
            QUEUE("sp", lambda tc=tc: win_emitted.__setitem__(tc, True))

    dead_keys = ["xt0", "xt1", "xn0", "xn1", "x2b0", "x2b1", "x2Ts0", "x2Ts1", "xnrb0", "xnrb1",
                 "scrA", "xnT0.0", "xnT0.1", "xnT1.0", "xnT1.1", "Wf_in", "ss1", "ss2",
                 "bcrow.0", "bcrow.1", "bcrow.2", "bcrow.3"]
    def _emit_fence1():
        sch.op("dve", lambda e: e.engine_nop(), reads=[], writes=dead_keys + ["fence1"],
               name="fence1")
    pend_low.append(("dve", _emit_fence1))
    sch.set_fence("fence1")
    sb_free("bcrow", "Wf_in", "xnT1", "xnT0", "scrA", "x2Ts1", "x2Ts0", "xnrb1", "xnrb0",
            "x2b1", "x2b0", "xn1", "xn0", "xt1", "xt0")
    NBLK = 16
    BS = 128
    Uring = [sb(f"Uring{i}", [128, 16 * 132], F32) for i in range(2)]
    gblk = [sb(f"gblk{i}", [128, 2048], F32) for i in range(2)]
    ACblk = [sb(f"ACblk{i}", [128, 2048], BF16) for i in range(2)]
    kblk = [sb(f"kblk{i}", [128, 2048], BF16) for i in range(2)]
    Wblk = [sb(f"Wblk{i}", [128, 2048], BF16) for i in range(2)]
    Vblk = sb("Vblk", [128, 2048], F32)
    Hblk = sb("Hblk", [128, 2048], F32)
    Hbb = sb("Hbb", [128, 2048], BF16)
    hprev = sb("hprev", [128, 16], F32)
    argb = [sb(f"argb{i}", [128, 16], F32) for i in range(2)]
    mb = [sb(f"mb{i}", [128, 16], F32) for i in range(2)]
    pb = [sb(f"pb{i}", [128, 16], F32) for i in range(2)]
    p2b = [sb(f"p2b{i}", [128, 16], F32) for i in range(2)]
    W_out_s = sb("W_out_s", [128, 8 * 1024], BF16)
    recst = [sb(f"recst{i}", [128, 512], BF16) for i in range(2)]
    ps_rec = ps_acc

    sch.dma(chW, "pool", W_out_s[:].rearrange("p (dt c) -> p dt c", dt=8),
            wd["W_out"].rearrange("(dt p) c -> p dt c", p=128),
            reads=["fence1"], writes=["W_out_s"], name="ld_Wout")
    sch.op("dve", lambda e: e.memset(hprev[:], 0.0), reads=["fence1"], writes=["hprev"], name="h0")

    def u_slice(ring, tloc, n=1):
        return ring[:].rearrange("p (g t) -> p g t", g=16)[:, :, tloc:tloc + n]

    def blk_slice(buf, tloc, n=1):
        return buf[:].rearrange("p (g t) -> p g t", g=16)[:, :, tloc:tloc + n]

    def col16(buf, tloc):
        return buf[:].rearrange("p (g t) -> p g t", g=16)[:, :, tloc:tloc + 1].rearrange("p g o -> p (g o)")

    pend = []
    pend_bg = []
    pend_tail = []

    def P(tag, fn):
        pend.append((tag, fn))

    def inject(n=1):
        for _ in range(n):
            if pend:
                pend.pop(0)[1]()
            elif pend_low:
                pend_low.pop(0)[1]()
            elif pend_bg:
                pend_bg.pop(0)[1]()
            elif pend_tail:
                pend_tail.pop(0)[1]()

    def inject_step():
        bud = dict(_BUD)
        for q in (pend, pend_low, pend_bg, pend_tail):
            while q:
                tag = q[0][0]
                if bud.get(tag, 0) <= 0:
                    break
                q.pop(0)[1]()
                bud[tag] -= 1
            if q:
                break

    def emit_pre(ib):
        emit_pre_dma(ib)
        emit_pre_comp(ib)

    def emit_pre_dma(ib):
        r = ib % 2
        t0 = ib * BS
        nread = 132 if ib < NBLK - 1 else 128
        tc_need = min(3, (t0 + nread - 1) // 512)
        while not win_emitted[tc_need] and pend_low:
            pend_low.pop(0)[1]()
        rk = []
        tcq = t0 // 512
        for dgb in range(16):
            dg, b = dgb // 2, dgb % 2
            rk.append(f"Ud.{dg}.{b}.{tcq}")
            if (t0 + nread) > (tcq + 1) * 512 and tcq < 3:
                rk.append(f"Ud.{dg}.{b}.{tcq + 1}")
        def du(r=r, t0=t0, nread=nread, rk=rk, ib=ib):
            sch.dma(chU, "sp",
                    Uring[r][:].rearrange("p (g t) -> p g t", g=16)[:, :, 0:nread],
                    U_dr.rearrange("p (g t) -> p g t", g=16)[:, :, t0:t0 + nread],
                    reads=rk + ["fence1"], writes=[f"Ur{r}"], name=f"ldU{ib}")
        P("sp", du)
        rk2 = [k.replace("Ud", "Gd") for k in rk]
        def dg_(r=r, t0=t0, rk2=rk2, ib=ib):
            sch.dma(chU, "sp",
                    gblk[r][:].rearrange("p (g t) -> p g t", g=16),
                    G_dr.rearrange("p (g t) -> p g t", g=16)[:, :, t0:t0 + 128],
                    reads=rk2 + ["fence1"], writes=[f"gb{r}"], name=f"ldG{ib}")
        P("sp", dg_)

    def emit_pre_comp(ib):
        r = ib % 2
        for q in range(8):
            def f1(r=r, q=q):
                sch.op("act", lambda e: e.activation(
                    gblk[r][:, q * 256:(q + 1) * 256], gblk[r][:, q * 256:(q + 1) * 256],
                    AF.Tanh, scale=cim[:, CI_H:CI_H + 1]),
                    reads=[f"gb{r}", "cimm"], writes=[f"gb{r}"], name=f"gt{ib}_{q}")
            P("act", f1)
        for q in range(4):
            def f2(r=r, q=q):
                sch.op("dve", lambda e: e.tensor_scalar(
                    gblk[r][:, q * 512:(q + 1) * 512], gblk[r][:, q * 512:(q + 1) * 512],
                    0.5, 0.5, ALU.mult, ALU.add),
                    reads=[f"gb{r}"], writes=[f"gb{r}"], name=f"gs{ib}_{q}")
            P("dve", f2)
        for gi in range(16):
            def f3a(r=r, gi=gi):
                sch.op("dve", lambda e, g2=gi: e.tensor_scalar(
                    ACblk[r][:, g2 * 128:(g2 + 1) * 128],
                    gblk[r][:, g2 * 128:(g2 + 1) * 128],
                    c_mA[:, g2 // 2:g2 // 2 + 1], c_A[:, g2 // 2:g2 // 2 + 1],
                    ALU.mult, ALU.add),
                    reads=[f"gb{r}", "consts"], writes=[f"ACb{r}"], name=f"ac{ib}_{gi}")
            P("dve", f3a)
        for q in range(4):
            def f4(r=r, q=q):
                sch.op("dve", lambda e: e.tensor_tensor(
                    blk_slice(kblk[r], 0, 128)[:, q * 4:(q + 1) * 4, :],
                    blk_slice(gblk[r], 0, 128)[:, q * 4:(q + 1) * 4, :],
                    u_slice(Uring[r], 0, 128)[:, q * 4:(q + 1) * 4, :], ALU.mult),
                    reads=[f"gb{r}", f"Ur{r}"], writes=[f"kb{r}"], name=f"gu{ib}_{q}")
            P("dve", f4)
        for q in range(4):
            def f5(r=r, q=q):
                sch.op("dve", lambda e: e.tensor_tensor(
                    blk_slice(kblk[r], 0, 128)[:, q * 4:(q + 1) * 4, :],
                    u_slice(Uring[r], 1, 128)[:, q * 4:(q + 1) * 4, :],
                    blk_slice(kblk[r], 0, 128)[:, q * 4:(q + 1) * 4, :], ALU.subtract),
                    reads=[f"kb{r}", f"Ur{r}"], writes=[f"kb{r}"], name=f"k{ib}_{q}")
            P("dve", f5)

    def emit_post(ib):
        r = ib % 2
        for q in range(4):
            def f1a(r=r, q=q):
                sch.op("dve", lambda e: e.tensor_tensor(
                    Vblk[:, q * 512:(q + 1) * 512], gblk[r][:, q * 512:(q + 1) * 512],
                    Wblk[r][:, q * 512:(q + 1) * 512], ALU.mult),
                    reads=[f"gb{r}", f"Wb{r}.f"], writes=["Vb", f"Vb.{q}"], name=f"Va{ib}_{q}")
            P("dve", f1a)
        for q in range(4):
            def f1b(r=r, q=q):
                sch.op("dve", lambda e: e.tensor_tensor(
                    Vblk[:, q * 512:(q + 1) * 512], Wblk[r][:, q * 512:(q + 1) * 512],
                    Vblk[:, q * 512:(q + 1) * 512], ALU.subtract),
                    reads=[f"Vb.{q}", f"Wb{r}.f"], writes=["Vb", f"Vb.{q}"], name=f"Vbp{ib}_{q}")
            P("dve", f1b)
        for dgb in range(16):
            def f2(r=r, g=dgb):
                sch.op("dve", lambda e: e.tensor_tensor_scan(
                    Hblk[:, g * 128:(g + 1) * 128], gblk[r][:, g * 128:(g + 1) * 128],
                    Vblk[:, g * 128:(g + 1) * 128], hprev[:, g:g + 1], ALU.mult, ALU.add),
                    reads=[f"gb{r}", f"Vb.{g // 4}", "hprev"], writes=[f"Hf.{g}"], name=f"tts{ib}_{g}")
            P("dve", f2)
        def f3():
            sch.op("dve", lambda e: e.tensor_copy(hprev[:], col16(Hblk, 127)),
                reads=[f"Hf.{g}" for g in range(16)], writes=["hprev"], name=f"hup{ib}")
        P("dve", f3)
        for q in range(8):
            def f4(q=q, ib=ib):
                sch.op("act", lambda e: e.activation(
                    Hbb[:, q * 256:(q + 1) * 256], Hblk[:, q * 256:(q + 1) * 256],
                    AF.Copy, scale=cim[:, CI_ONE:CI_ONE + 1]),
                    reads=[f"Hf.{g}" for g in range(q * 2, q * 2 + 2)] + ["cimm"],
                    writes=["Hbb", f"Hbb.{q}"], name=f"hbb{ib}_{q}")
            P("act", f4)
        for b in range(BPC):
            for ch in range(2):
                bank = (b * 2 + ch) % 2
                def f5(b=b, ch=ch, bank=bank, ib=ib):
                    for dg in range(8):
                        sch.op("pe", lambda e, dg=dg: e.matmul(
                            ps_rec[bank][:],
                            Hbb[:, (dg * 2 + b) * 128:(dg * 2 + b + 1) * 128],
                            W_out_s[:, dg * 1024 + ch * 512:dg * 1024 + (ch + 1) * 512],
                            start=(dg == 0), stop=(dg == 7)),
                            reads=[f"Hbb.{(dg * 2 + b) // 2}", "W_out_s"],
                            writes=[f"pk{bank}"], name=f"rec{ib}_{b}{ch}_{dg}")
                P("pe", f5)
                for hc in range(2):
                    def f6(b=b, ch=ch, bank=bank, ib=ib, hc=hc):
                        sch.op("act", lambda e: e.activation(
                            recst[bank][:, hc * 256:(hc + 1) * 256],
                            ps_rec[bank][:, hc * 256:(hc + 1) * 256],
                            AF.Copy, scale=cim[:, CI_ONE:CI_ONE + 1]),
                            reads=[f"pk{bank}", "cimm"], writes=[f"recst{bank}"],
                            name=f"recev{ib}_{b}{ch}_{hc}")
                    P("act", f6)
                def f7(b=b, ch=ch, bank=bank, ib=ib):
                    sch.dma(chS, "sp",
                            rec_dr[b * 2048 + ib * 128:b * 2048 + (ib + 1) * 128, ch * 512:(ch + 1) * 512],
                            recst[bank][:], reads=[f"recst{bank}"],
                            writes=[f"recd.{ib}.{b}.{ch}"], name=f"recst{ib}_{b}{ch}")
                P("sp", f7)


    BG = pend_bg

    _BGT = [pend_bg]

    def B(tag, fn):
        _BGT[0].append((tag, fn))

    WqkvT_s = sb("WqkvT_s", [128, 8 * 1536], BF16, side="right")
    lnb_b = sb("lnb_b", [128, 8], BF16, side="right")
    bqrow = sb("bqrow", [1, 1536], F32, side="right")
    Bqk_pp = sb("Bqk_pp", [128, 8], F32, side="right")
    Bv_bc = sb("Bv_bc", [128, 512], F32, side="right")
    ones1 = sb("ones1", [1, 128], F32, side="right")
    ones_col = sb("ones_col", [128, 1], BF16, side="right")
    kTfull = sb("kTfull", [128, 4 * 2048], BF16, side="right")
    vfull = sb("vfull", [128, 17 * 520], BF16, side="right")
    qring = [sb(f"qring{i}", [128, 4 * 512], BF16, side="right") for i in range(2)]
    probsL = [sb(f"probsL{i}", [128, 256], BF16, side="right") for i in range(2)]
    avnL = sb("avnL", [128, 512], BF16, side="right")
    avlst = sb("avlst", [128, 512], BF16, side="right")
    rec8 = sb("rec8", [128, 8], F32, side="right")

    def gen_light():
        _BGT[0] = pend_low
        B("pool", lambda: sch.dma(chW, "pool", WqkvT_s[:].rearrange("p (dt i) -> p dt i", dt=8),
                  wd["WqkvT"].rearrange("(dt p) i -> p dt i", p=128),
                  writes=["Wqkv_s"], name="ld_Wqkv"))
        B("dve", lambda: sch.op("dve", lambda e: e.tensor_copy(lnb_b[:], c_lnb),
                  reads=["consts"], writes=["lnb_b"], name="lnbb"))
        B("dve", lambda: sch.op("dve", lambda e: e.memset(ones1[:], 1.0),
                  writes=["ones1"], name="ones1"))
        B("dve", lambda: sch.op("dve", lambda e: e.memset(ones_col[:], 1.0),
                  writes=["ones_col"], name="ones_col"))
        for cc in range(3):
            for dt in range(8):
                def f(cc=cc, dt=dt):
                    sch.op("pe", lambda e: e.matmul(
                        ps_c[0:1, :], lnb_b[:, dt:dt + 1],
                        WqkvT_s[:, dt * 1536 + cc * 512:dt * 1536 + (cc + 1) * 512],
                        start=(dt == 0), stop=(dt == 7)),
                        reads=["lnb_b", "Wqkv_s"], writes=["pk7"], name=f"bq{cc}_{dt}")
                B("pe", f)
            def f2(cc=cc):
                sch.op("dve", lambda e: e.tensor_copy(bqrow[:, cc * 512:(cc + 1) * 512], ps_c[0:1, :]),
                       reads=["pk7"], writes=["bqrow"], name=f"bqev{cc}")
            B("dve", f2)
        for it in range(8):
            def f3(it=it):
                sch.op("pe", lambda e: e.matmul(
                    ps_r[:, it:it + 1], bqrow[:, it * 128:(it + 1) * 128], ident_f[0:1, 0:1],
                    start=True, stop=True),
                    reads=["bqrow", "identf"], writes=["pk7"], name=f"bqtr{it}")
            B("pe", f3)
        B("dve", lambda: sch.op("dve", lambda e: e.tensor_copy(Bqk_pp[:], ps_r[:, 0:8]),
                  reads=["pk7"], writes=["Bqk_pp"], name="bqev2"))
        B("pe", lambda: sch.op("pe", lambda e: e.matmul(
            ps_misc[:], ones1[:], bqrow[0:1, 1024:1536], start=True, stop=True),
            reads=["ones1", "bqrow"], writes=["pk7"], name="bvbc"))
        B("dve", lambda: sch.op("dve", lambda e: e.tensor_copy(Bv_bc[:], ps_misc[:]),
                  reads=["pk7"], writes=["Bv_bc"], name="bvev"))
        for dt in range(8):
            for cc3 in range(3):
                def f4(dt=dt, cc3=cc3):
                    sch.op("dve", lambda e: e.tensor_scalar(
                        WqkvT_s[:, dt * 1536 + cc3 * 512:dt * 1536 + (cc3 + 1) * 512],
                        WqkvT_s[:, dt * 1536 + cc3 * 512:dt * 1536 + (cc3 + 1) * 512],
                        c_lnw[:, dt:dt + 1], None, ALU.mult),
                        reads=["Wqkv_s", "consts", "bqrow"], writes=["Wqkv_s"], name=f"qf{dt}_{cc3}")
                B("dve", f4)
        B("dve", lambda: sch.op("dve", lambda e: e.memset(vfull[0:64, 0:520], 0.0),
                  writes=["vt.0.lo"], name="vz0"))
        B("dve", lambda: sch.op("dve", lambda e: e.memset(vfull[64:128, 16 * 520:17 * 520], 0.0),
                  writes=["vt.16.hi"], name="vz16"))
        B("dve", lambda: sch.op("dve", lambda e: e.memset(
                  vfull[:].rearrange("p (s i) -> p s i", i=65)[:, :, 64:65], 1.0),
                  reads=["vt.0.lo", "vt.16.hi"], writes=["vones"], name="vzone"))

        _BGT[0] = pend_bg
        for b in range(BPC):
            next_mt = [0]
            for tc in range(4):
                _BGT[0] = pend_low if (b == 0 and tc <= 1) else pend_bg
                sl = (b * 4 + tc) % 2
                jlist = [b * 16 + tc * 4 + q for q in range(4)]
                def fld(b=b, tc=tc, sl=sl, jlist=jlist):
                    sch.dma(chU, "sp",
                            x2c[sl][:].rearrange("p (dt t) -> p dt t", dt=8),
                            xn2T_dr.rearrange("p (dt t) -> p dt t", dt=8)[:, :, (b * 2048 + tc * 512):(b * 2048 + (tc + 1) * 512)],
                            reads=[f"x2Td.{j}" for j in jlist], writes=[f"x2c{sl}"], name=f"lx2c{b}_{tc}")
                B("sp", fld)
                for it in range(8):
                    bank = 2 + it % 2
                    def fmm(it=it, sl=sl, bank=bank, b=b, tc=tc):
                        for dt in range(8):
                            sch.op("pe", lambda e, dt=dt: e.matmul(
                                ps_acc[bank][:],
                                WqkvT_s[:, dt * 1536 + it * 128:dt * 1536 + (it + 1) * 128],
                                x2c[sl][:, dt * 512:(dt + 1) * 512],
                                start=(dt == 0), stop=(dt == 7)),
                                reads=["Wqkv_s", f"x2c{sl}"], writes=[f"pk{bank}"], name=f"qk{b}{tc}_{it}_{dt}")
                    B("pe", fmm)
                    if it < 4:
                        def fev(it=it, sl=sl, bank=2 + it % 2, tc=tc, b=b):
                            sch.op("dve", lambda e: e.tensor_scalar(
                                qring[sl][:, it * 512:(it + 1) * 512], ps_acc[bank][:],
                                Bqk_pp[:, it:it + 1], None, ALU.add),
                                reads=[f"pk{bank}", "Bqk_pp"], writes=[f"qr{sl}"], name=f"qev{b}{tc}_{it}")
                        B("dve", fev)
                    else:
                        def fev(it=it, sl=sl, bank=2 + it % 2, tc=tc, b=b):
                            sch.op("dve", lambda e: e.tensor_scalar(
                                kTfull[:, (it - 4) * 2048 + tc * 512:(it - 4) * 2048 + (tc + 1) * 512],
                                ps_acc[bank][:], Bqk_pp[:, it:it + 1], None, ALU.add),
                                reads=[f"pk{bank}", "Bqk_pp"], writes=[f"kT.{tc}"], name=f"kev{b}{tc}_{it}")
                        B("dve", fev)
                vjobs = []
                for m in range(1, 4):
                    vt = 4 * tc + m
                    vjobs.append((vt, "full", 128 * m - 64, 128))
                vjobs.append((4 * tc, "hi", 0, 64))
                vjobs.append((4 * tc + 4, "lo", 448, 64))
                for vj, (vt, kind, off, mlen) in enumerate(vjobs):
                    bank = 2 + vj % 2
                    def fvm(vt=vt, kind=kind, off=off, mlen=mlen, sl=sl, bank=bank, b=b, tc=tc):
                        pbase = 64 if kind == "hi" else 0
                        for dt in range(8):
                            sch.op("pe", lambda e, dt=dt: e.matmul(
                                ps_acc[bank][pbase:pbase + mlen, :],
                                x2c[sl][:, dt * 512 + off:dt * 512 + off + mlen],
                                WqkvT_s[:, dt * 1536 + 1024:dt * 1536 + 1536],
                                start=(dt == 0), stop=(dt == 7)),
                                reads=[f"x2c{sl}", "Wqkv_s"], writes=[f"pk{bank}"], name=f"v{b}{tc}_{vt}{kind}_{dt}")
                    B("pe", fvm)
                    def fve(vt=vt, kind=kind, mlen=mlen, bank=2 + vj % 2, b=b):
                        pbase = 64 if kind == "hi" else 0
                        sch.op("dve", lambda e: e.tensor_tensor(
                            vfull[pbase:pbase + mlen, vt * 520:(vt + 1) * 520].rearrange(
                                "p (h i) -> p h i", i=65)[:, :, 0:64],
                            ps_acc[bank][pbase:pbase + mlen, :].rearrange("p (h i) -> p h i", h=8),
                            Bv_bc[pbase:pbase + mlen, :].rearrange("p (h i) -> p h i", h=8), ALU.add),
                            reads=[f"pk{bank}", "Bv_bc"], writes=[f"vt.{vt}.{'hi' if kind == 'hi' else 'lo'}" if kind != "full" else f"vt.{vt}.full"],
                            name=f"vev{b}{tc}_{vt}{kind}")
                    B("dve", fve)
                _BGT[0] = pend_bg
                hi_mt = 15 if tc == 3 else 4 * tc + 2
                for mt in range(next_mt[0], hi_mt + 1):
                    gen_light_attn_mt(b, mt, (b * 4 + mt // 4) % 2)
                next_mt[0] = hi_mt + 1

    def gen_light_attn_mt(b, mt, qsl):
        if True:
            tc, mi = mt // 4, mt % 4
            tkA = max(0, 128 * mt - 64)
            lenB = 128 if 128 * mt + 192 <= 2048 else 64
            vdeps = []
            for vt in (mt, mt + 1):
                if vt == 0:
                    vdeps += ["vt.0.lo", "vt.0.hi"]
                elif vt == 16:
                    vdeps += ["vt.16.lo", "vt.16.hi"]
                elif vt % 4 == 0:
                    vdeps += [f"vt.{vt}.lo", f"vt.{vt}.hi"]
                else:
                    vdeps += [f"vt.{vt}.full"]
            kdeps = [f"kT.{tc2}" for tc2 in {max(0, (tkA) // 512), min(3, (tkA + 128 + lenB - 1) // 512)}]
            prev_fav = None
            for h in range(8):
                pb, itq = (h % 2) * 64, h // 2
                hs = h % 2
                qread = f"qr{qsl}"
                def fsA(h=h, pb=pb, itq=itq, mt=mt, mi=mi, tkA=tkA, qsl=qsl, kdeps=kdeps, qread=qread):
                    if mt == 0:
                        sch.op("pe", lambda e: e.matmul(
                            ps_f[0][64:128, 0:128],
                            kTfull[pb:pb + 64, (h // 2) * 2048:(h // 2) * 2048 + 64],
                            qring[qsl][pb:pb + 64, (h // 2) * 512 + mi * 128:(h // 2) * 512 + (mi + 1) * 128],
                            start=True, stop=True),
                            reads=kdeps + [qread], writes=["pk5"], name=f"sA{b}_{mt}_{h}")
                        return
                    sch.op("pe", lambda e: e.matmul(
                        ps_f[0][:, 0:128],
                        kTfull[pb:pb + 64, (h // 2) * 2048 + tkA:(h // 2) * 2048 + tkA + 128],
                        qring[qsl][pb:pb + 64, (h // 2) * 512 + mi * 128:(h // 2) * 512 + (mi + 1) * 128],
                        start=True, stop=True),
                        reads=kdeps + [qread], writes=["pk5"], name=f"sA{b}_{mt}_{h}")
                B("pe", fsA)
                def fsB(h=h, pb=pb, mt=mt, mi=mi, lenB=lenB, qsl=qsl, kdeps=kdeps, qread=qread):
                    tkB = 128 * mt + 64
                    sch.op("pe", lambda e: e.matmul(
                        ps_f[1][0:lenB, 0:128],
                        kTfull[pb:pb + 64, (h // 2) * 2048 + tkB:(h // 2) * 2048 + tkB + lenB],
                        qring[qsl][pb:pb + 64, (h // 2) * 512 + mi * 128:(h // 2) * 512 + (mi + 1) * 128],
                        start=True, stop=True),
                        reads=kdeps + [qread], writes=["pk6"], name=f"sB{b}_{mt}_{h}")
                B("pe", fsB)
                if prev_fav is not None:
                    B("pe", prev_fav)
                def fxA(mt=mt, b=b, h=h, hs=hs):
                    if mt == 0:
                        sch.op("act", lambda e: e.activation(
                            probsL[hs][64:128, 0:128], ps_f[0][64:128, 0:128], AF.Exp,
                            scale=cim[64:128, 3:4]),
                            reads=["pk5", "cimm"], writes=[f"prA{hs}"], name=f"eA{b}_{mt}_{h}")
                        return
                    sch.op("act", lambda e: e.activation(
                        probsL[hs][:, 0:128], ps_f[0][:, 0:128], AF.Exp, scale=cim[:, 3:4]),
                        reads=["pk5", "cimm"], writes=[f"prA{hs}"], name=f"eA{b}_{mt}_{h}")
                B("act", fxA)
                def fxB(mt=mt, b=b, h=h, lenB=lenB, hs=hs):
                    sch.op("act", lambda e: e.activation(
                        probsL[hs][0:lenB, 128:256], ps_f[1][0:lenB, 0:128], AF.Exp, scale=cim[0:lenB, 3:4]),
                        reads=["pk6", "cimm"], writes=[f"prB{hs}"], name=f"eB{b}_{mt}_{h}")
                B("act", fxB)
                wl = []
                if mt == 0:
                    wl.append(("prA", 0, 64, 0, 128))
                    wl.append(("prB", 64, 128, 0, 64))
                else:
                    wl.append(("prA", 0, 64, 64, 128))
                    if lenB == 128:
                        wl.append(("prB", 64, 128, 0, 64))
                    else:
                        wl.append(("prB", 64, 128, 0, 128))
                for wkey, p0, p1, c0, c1 in wl:
                    coff = 0 if wkey == "prA" else 128
                    def fw2(p0=p0, p1=p1, c0=c0, c1=c1, coff=coff, wkey=wkey, mt=mt, h=h, b=b, hs=hs):
                        sch.op("dve", lambda e: e.memset(probsL[hs][p0:p1, coff + c0:coff + c1], 0.0),
                               reads=[], writes=[wkey + str(hs)], name=f"wz{b}_{mt}_{h}")
                    B("dve", fw2)
                if mt == 15 and lenB == 64:
                    def fw3(b=b, h=h, hs=hs):
                        sch.op("dve", lambda e: e.memset(probsL[hs][64:128, 128:256], 0.0),
                               writes=[f"prB{hs}"], name=f"wzt{b}_{h}")
                    B("dve", fw3)
                def favAll(mt=mt, h=h, b=b, vdeps=vdeps, hs=hs):
                    bk = 2 + h // 4
                    co = (h % 4) * 65
                    sch.op("pe", lambda e, bk=bk, co=co: e.matmul(
                        ps_acc[bk][:, co:co + 65], probsL[hs][:, 0:128],
                        vfull[:, mt * 520 + h * 65:mt * 520 + (h + 1) * 65],
                        start=True, stop=False),
                        reads=[f"prA{hs}", "vones"] + vdeps, writes=[f"pk{bk}"], name=f"avA{b}_{mt}_{h}")
                    sch.op("pe", lambda e, bk=bk, co=co: e.matmul(
                        ps_acc[bk][:, co:co + 65], probsL[hs][:, 128:256],
                        vfull[:, (mt + 1) * 520 + h * 65:(mt + 1) * 520 + (h + 1) * 65],
                        start=False, stop=True),
                        reads=[f"prB{hs}", "vones"] + vdeps, writes=[f"pk{bk}"], name=f"avB{b}_{mt}_{h}")
                prev_fav = favAll
            B("pe", prev_fav)
            def frc(b=b, mt=mt):
                sch.op("dve", lambda e: e.reciprocal(
                    rec8[:, 0:4], ps_acc[2][:, 0:260].rearrange("p (h i) -> p h i", i=65)[:, :, 64]),
                    reads=["pk2"], writes=["rec8"], name=f"rc2{b}_{mt}")
                sch.op("dve", lambda e: e.reciprocal(
                    rec8[:, 4:8], ps_acc[3][:, 0:260].rearrange("p (h i) -> p h i", i=65)[:, :, 64]),
                    reads=["pk3"], writes=["rec8"], name=f"rc3{b}_{mt}")
            B("dve", frc)
            for h in range(8):
                def fn(h=h, b=b, mt=mt):
                    bk = 2 + h // 4
                    co = (h % 4) * 65
                    sch.op("dve", lambda e, bk=bk, co=co: e.tensor_scalar(
                        avnL[:, h * 64:(h + 1) * 64], ps_acc[bk][:, co:co + 64],
                        rec8[:, h:h + 1], None, ALU.mult),
                        reads=[f"pk{bk}", "rec8"], writes=["avnL"], name=f"nr{b}_{mt}_{h}")
                B("dve", fn)
            for ip in range(4):
                def ft(ip=ip, b=b, mt=mt):
                    sch.op("pe", lambda e: e.transpose(
                        ps_bT[:, ip * 128:(ip + 1) * 128], avnL[:, ip * 128:(ip + 1) * 128], ident_b[:]),
                        reads=["avnL", "identb"], writes=["pk4"], name=f"avt{b}_{mt}_{ip}")
                B("pe", ft)
            B("dve", lambda b=b, mt=mt: sch.op("dve", lambda e: e.tensor_copy(avlst[:], ps_bT[:, 0:512]),
                      reads=["pk4"], writes=["avlst"], name=f"avte{b}_{mt}"))
            def fst(b=b, mt=mt):
                sch.dma(chS, "sp",
                        avlT_dr.rearrange("p (ip t) -> p ip t", ip=4)[:, :, b * 2048 + mt * 128:b * 2048 + (mt + 1) * 128],
                        avlst[:].rearrange("p (ip t) -> p ip t", ip=4),
                        reads=["avlst"], writes=[f"avld.{b}.{mt}"], name=f"avst{b}_{mt}")
            B("sp", fst)


    S16 = sb("S16", [128, 64], F32, side="right")
    R16 = sb("R16", [128, 64], F32, side="right")
    X16 = sb("X16", [128, 64], F32, side="right")
    E16 = sb("E16", [128, 64], F32, side="right")
    L16 = sb("L16", [128, 64], F32, side="right")
    T16 = sb("T16", [128, 64], F32, side="right")
    D16 = sb("D16", [128, 64], F32, side="right")
    I16 = sb("I16", [128, 64], I32, side="right")
    cs64 = sb("cs64", [128, 64], F32, side="right")
    O64 = sb("O64", [128, 64], F32, side="right")
    smallt = sb("smallt", [128, 8], F32, side="right")
    smalli = sb("smalli", [128, 6], I32, side="right")
    pm1 = sb("pm1", [1, 128], F32, side="right")
    t4row = sb("t4row", [1, 4], F32, side="right")
    m4 = sb("m4", [4, 1], F32, side="right")
    S4sb = sb("S4sb", [4, 1], F32, side="right")
    lsej = sb("lsej", [4, 1], F32, side="right")
    t41 = sb("t41", [4, 1], F32, side="right")
    ne4 = sb("ne4", [4, 1], F32, side="right")
    a4 = sb("a4", [4, 1], F32, side="right")
    apos = sb("apos", [128, 1], F32, side="right")
    negm = sb("negm", [128, 1], F32, side="right")
    asg4_s = sb("asg4_s", [4, 128], F32, side="right")
    asg4T_s = sb("asg4T_s", [128, 4], F32, side="right")
    ltri_s = sb("ltri_s", [128, 128], F32, side="right")
    logk4_s = sb("logk4_s", [4, 1], F32, side="right")
    kpp_s = sb("kpp_s", [128, 1], F32, side="right")
    rbase_s = sb("rbase_s", [128, 1], F32, side="right")
    xgt = [sb(f"xgt{i}", [128, D + 4], BF16, side="right") for i in range(2)]
    qic = [sb(f"qic{i}", [128, 1], I32, side="right") for i in range(2)]

    def R(eng, fn, reads=(), writes=(), name=""):
        B(eng, lambda: sch.op(eng, fn, reads, writes, name))

    rti_dr = scratch("rt_it", [4, 200], F32) if "rt_it" in dbg_names else None

    def gen_route():
        B("sp", lambda: sch.dma(chL, "sp", asg4_s[:], wd["asg4"], writes=["asg4"], name="ld_a4"))
        B("sp", lambda: sch.dma(chL, "sp", asg4T_s[:], wd["asg4T"], writes=["asg4T"], name="ld_a4T"))
        B("sp", lambda: sch.dma(chL, "sp", ltri_s[:], wd["ltri"], writes=["ltri"], name="ld_lt"))
        B("sp", lambda: sch.dma(chL, "sp", logk4_s[:], wd["logk4"], writes=["logk4"], name="ld_lk"))
        B("sp", lambda: sch.dma(chL, "sp", kpp_s[:], wd["kpp"], writes=["kpp"], name="ld_kp"))
        B("sp", lambda: sch.dma(chL, "sp", rbase_s[:], wd["rbase"], writes=["rbase"], name="ld_rb"))
        B("sp", lambda: sch.dma(chL, "sp", S16[:],
                  s_dr.rearrange("r (b jj f) -> (r b jj) f", b=2, f=64),
                  reads=[f"s_dr.{q}" for q in range(8)], writes=["S16"], name="ld_S16"))
        R("dve", lambda e: e.tensor_copy(R16[:], S16[:]), ["S16"], ["R16"], "r16i")
        R("dve", lambda e: e.memset(lsej[:], float(np.log(2048.0))), [], ["lsej"], "lsei")
        R("dve", lambda e: e.memset(O64[:], 1.0), [], ["O64"], "o64")
        for it in range(NITERS):
            R("dve", lambda e: e.tensor_sub(X16[:], S16[:], R16[:]), ["S16", "R16"], ["X16"], f"rx{it}")
            R("dve", lambda e: e.tensor_reduce(smallt[:, 0:1], X16[:], AX, ALU.max),
              ["X16"], ["pmax"], f"rm{it}")
            R("pe", lambda e: e.transpose(ps_misc[0:1, 0:128], smallt[:, 0:1], ident_f[:]),
              ["pmax", "identf"], ["pk7"], f"rt{it}")
            R("dve", lambda e: e.tensor_copy(pm1[:], ps_misc[0:1, 0:128]), ["pk7"], ["pm1"], f"rte{it}")
            R("dve", lambda e: e.tensor_reduce(t4row[:], pm1[:].rearrange("o (g j) -> o g j", g=4), AX, ALU.max),
              ["pm1"], ["t4row"], f"rg{it}")
            R("pe", lambda e: e.matmul(ps_misc[0:4, 0:1], t4row[:], ident_f[0:1, 0:1], start=True, stop=True),
              ["t4row", "identf"], ["pk7"], f"rtm{it}")
            R("dve", lambda e: e.tensor_copy(m4[:], ps_misc[0:4, 0:1]), ["pk7"], ["m4"], f"rtme{it}")
            R("pe", lambda e: e.matmul(ps_misc[:, 1:2], asg4_s[:], m4[:], start=True, stop=True),
              ["asg4", "m4"], ["pk7"], f"rbc{it}")
            R("dve", lambda e: e.tensor_scalar(negm[:], ps_misc[:, 1:2], -1.0, None, ALU.mult),
              ["pk7"], ["negm"], f"rbe{it}")
            R("act", lambda e: e.activation(E16[:], X16[:], AF.Exp, bias=negm[:], accum_out=smallt[:, 1:2]),
              ["X16", "negm"], ["E16", "esum"], f"re{it}")
            R("pe", lambda e: e.matmul(ps_misc[0:4, 2:3], asg4T_s[:], smallt[:, 1:2], start=True, stop=True),
              ["asg4T", "esum"], ["pk7"], f"rs{it}")
            R("dve", lambda e: e.tensor_copy(S4sb[:], ps_misc[0:4, 2:3]), ["pk7"], ["S4sb"], f"rse{it}")
            for nw in range(4):
                R("act", lambda e: e.activation(ne4[:], lsej[:], AF.Exp, scale=cim[0:4, 5:6]),
                  ["lsej", "cimm"], ["ne4"], f"rn{it}_{nw}")
                R("dve", lambda e: e.tensor_mul(t41[:], S4sb[:], ne4[:]), ["S4sb", "ne4"], ["t41"], f"rnm{it}_{nw}")
                R("dve", lambda e: e.tensor_scalar(t41[:], t41[:], -1.0, None, ALU.add),
                  ["t41"], ["t41"], f"rna{it}_{nw}")
                R("dve", lambda e: e.tensor_add(lsej[:], lsej[:], t41[:]), ["lsej", "t41"], ["lsej"], f"rnu{it}_{nw}")
            R("dve", lambda e: e.tensor_add(t41[:], m4[:], lsej[:]), ["m4", "lsej"], ["t41"], f"ra{it}")
            R("dve", lambda e: e.tensor_sub(a4[:], logk4_s[:], t41[:]), ["logk4", "t41"], ["a4"], f"ra2{it}")
            R("pe", lambda e: e.matmul(ps_misc[:, 3:4], asg4_s[:], a4[:], start=True, stop=True),
              ["asg4", "a4"], ["pk7"], f"rab{it}")
            R("dve", lambda e: e.tensor_copy(apos[:], ps_misc[:, 3:4]), ["pk7"], ["apos"], f"rabe{it}")
            R("act", lambda e: e.activation(R16[:], S16[:], AF.Relu, bias=apos[:]),
              ["S16", "apos"], ["R16"], f"rr{it}")
            if "rt_it" in dbg_names:
                def dit(it=it):
                    for c, (ap, rk) in enumerate([(a4, "a4"), (lsej, "lsej"),
                                                  (m4, "m4"), (S4sb, "S4sb")]):
                        def op(e, ap=ap, c=c, it=it):
                            return e.dma_start(rti_dr[0:4, 4 * it + c:4 * it + c + 1],
                                               ap[:], allow_slow_non_contiguous=True)
                        sch.op("sp", op, reads=[rk], writes=["rti"],
                               name=f"di{c}_{it}", is_dma=True, dma_ch=chS)
                B("sp", dit)
        R("dve", lambda e: e.tensor_scalar(X16[:], S16[:], apos[:], None, ALU.add),
          ["S16", "apos"], ["X16"], "sel_sa")
        R("dve", lambda e: e.tensor_sub(X16[:], X16[:], R16[:]), ["X16", "R16"], ["X16"], "sel_t")
        R("dve", lambda e: e.tensor_scalar(E16[:], X16[:], -1.0, None, ALU.mult),
          ["X16"], ["Y16", "E16"], "sel_y")
        R("dve", lambda e: e.memset(smalli[:, 0:1], 0), [], ["lo_"], "sel_lo")
        R("dve", lambda e: e.memset(smalli[:, 1:2], 2139095039), [], ["hi_"], "sel_hi")
        for it in range(31):
            R("dve", lambda e: e.tensor_add(smalli[:, 2:3], smalli[:, 0:1], smalli[:, 1:2]),
              ["lo_", "hi_"], ["mid_"], f"bs_a{it}")
            R("dve", lambda e: e.tensor_scalar(smalli[:, 2:3], smalli[:, 2:3], 1, None, ALU.arith_shift_right),
              ["mid_"], ["mid_"], f"bs_s{it}")
            R("dve", lambda e: e.tensor_scalar(T16[:], E16[:], smalli[:, 2:3].bitcast(F32), None, ALU.is_le),
              ["Y16", "mid_"], ["cmp_"], f"bs_c{it}")
            R("dve", lambda e: e.tensor_reduce(smallt[:, 2:3], T16[:], AX, ALU.add),
              ["cmp_"], ["cnt_"], f"bs_r{it}")
            R("pe", lambda e: e.matmul(ps_misc[0:4, 4:5], asg4T_s[:], smallt[:, 2:3], start=True, stop=True),
              ["asg4T", "cnt_"], ["pk7"], f"bs_g{it}")
            R("dve", lambda e: e.tensor_copy(t41[:], ps_misc[0:4, 4:5]), ["pk7"], ["c4_"], f"bs_ge{it}")
            R("pe", lambda e: e.matmul(ps_misc[:, 5:6], asg4_s[:], t41[:], start=True, stop=True),
              ["asg4", "c4_"], ["pk7"], f"bs_b{it}")
            R("dve", lambda e: e.tensor_copy(smallt[:, 3:4], ps_misc[:, 5:6]), ["pk7"], ["cpp_"], f"bs_be{it}")
            R("dve", lambda e: e.tensor_tensor(smallt[:, 4:5], smallt[:, 3:4], kpp_s[:], ALU.is_ge),
              ["cpp_", "kpp"], ["cge_"], f"bs_ge2{it}")
            R("dve", lambda e: e.tensor_copy(smalli[:, 3:4], smallt[:, 4:5]), ["cge_"], ["cgei_"], f"bs_ci{it}")
            R("dve", lambda e: e.tensor_scalar(smalli[:, 4:5], smalli[:, 3:4], -1, 1, ALU.mult, ALU.add),
              ["cgei_"], ["ncge_"], f"bs_n{it}")
            R("dve", lambda e: e.tensor_sub(smalli[:, 5:6], smalli[:, 2:3], smalli[:, 1:2]),
              ["mid_", "hi_"], ["tmp_"], f"bs_d1{it}")
            R("dve", lambda e: e.tensor_mul(smalli[:, 5:6], smalli[:, 5:6], smalli[:, 3:4]),
              ["tmp_", "cgei_"], ["tmp_"], f"bs_d2{it}")
            R("dve", lambda e: e.tensor_add(smalli[:, 1:2], smalli[:, 1:2], smalli[:, 5:6]),
              ["hi_", "tmp_"], ["hi_"], f"bs_d3{it}")
            R("dve", lambda e: e.tensor_scalar(smalli[:, 5:6], smalli[:, 2:3], 1, None, ALU.add),
              ["mid_"], ["tmp_"], f"bs_e1{it}")
            R("dve", lambda e: e.tensor_sub(smalli[:, 5:6], smalli[:, 5:6], smalli[:, 0:1]),
              ["tmp_", "lo_"], ["tmp_"], f"bs_e2{it}")
            R("dve", lambda e: e.tensor_mul(smalli[:, 5:6], smalli[:, 5:6], smalli[:, 4:5]),
              ["tmp_", "ncge_"], ["tmp_"], f"bs_e3{it}")
            R("dve", lambda e: e.tensor_add(smalli[:, 0:1], smalli[:, 0:1], smalli[:, 5:6]),
              ["lo_", "tmp_"], ["lo_"], f"bs_e4{it}")
        thf = smalli[:, 0:1].bitcast(F32)
        R("dve", lambda e: e.tensor_scalar(L16[:], E16[:], thf, None, ALU.is_lt),
          ["Y16", "lo_"], ["L16"], "sel_lt")
        R("dve", lambda e: e.tensor_reduce(smallt[:, 2:3], L16[:], AX, ALU.add),
          ["L16"], ["cnt_"], "sel_cr")
        R("pe", lambda e: e.matmul(ps_misc[0:4, 4:5], asg4T_s[:], smallt[:, 2:3], start=True, stop=True),
          ["asg4T", "cnt_"], ["pk7"], "sel_cg")
        R("dve", lambda e: e.tensor_copy(t41[:], ps_misc[0:4, 4:5]), ["pk7"], ["c4_"], "sel_cge")
        R("pe", lambda e: e.matmul(ps_misc[:, 5:6], asg4_s[:], t41[:], start=True, stop=True),
          ["asg4", "c4_"], ["pk7"], "sel_cb")
        R("dve", lambda e: e.tensor_copy(smallt[:, 3:4], ps_misc[:, 5:6]), ["pk7"], ["cltpp_", "cpp_"], "sel_cbe")
        R("dve", lambda e: e.tensor_sub(smallt[:, 4:5], kpp_s[:], smallt[:, 3:4]),
          ["kpp", "cltpp_"], ["need_", "cge_"], "sel_nd")
        R("dve", lambda e: e.tensor_scalar(T16[:], E16[:], thf, None, ALU.is_equal),
          ["Y16", "lo_"], ["tie_", "cmp_"], "sel_tie")
        R("dve", lambda e: e.tensor_tensor_scan(cs64[:], O64[:], T16[:], 0.0, ALU.mult, ALU.add),
          ["O64", "tie_"], ["cs_"], "sel_cs")
        R("dve", lambda e: e.tensor_reduce(smallt[:, 5:6], T16[:], AX, ALU.add),
          ["tie_"], ["tot_"], "sel_tr")
        R("pe", lambda e: e.matmul(ps_misc[:, 6:7], ltri_s[:], smallt[:, 5:6], start=True, stop=True),
          ["ltri", "tot_"], ["pk7"], "sel_px")
        R("dve", lambda e: e.tensor_copy(smallt[:, 6:7], ps_misc[:, 6:7]), ["pk7"], ["pfx_"], "sel_pxe")
        R("dve", lambda e: e.tensor_scalar(cs64[:], cs64[:], smallt[:, 6:7], None, ALU.add),
          ["cs_", "pfx_"], ["cs_"], "sel_rk")
        R("dve", lambda e: e.tensor_scalar(T16[:], cs64[:], smallt[:, 4:5], None, ALU.is_le),
          ["cs_", "need_", "tie_"], ["sle_", "tie_"], "sel_le")
        R("dve", lambda e: e.tensor_scalar(E16[:], E16[:], thf, None, ALU.is_equal),
          ["Y16", "lo_"], ["tie2_", "Y16"], "sel_tie2")
        R("dve", lambda e: e.tensor_mul(T16[:], T16[:], E16[:]), ["sle_", "tie2_"], ["sle_"], "sel_and")
        R("dve", lambda e: e.tensor_add(L16[:], L16[:], T16[:]), ["L16", "sle_"], ["SEL_", "L16"], "sel_or")
        R("dve", lambda e: e.tensor_tensor_scan(cs64[:], O64[:], L16[:], 0.0, ALU.mult, ALU.add),
          ["O64", "SEL_"], ["cs2_", "cs_"], "sel_cs2")
        R("dve", lambda e: e.tensor_reduce(smallt[:, 5:6], L16[:], AX, ALU.add),
          ["SEL_"], ["tot2_", "tot_"], "sel_tr2")
        R("pe", lambda e: e.matmul(ps_misc[:, 6:7], ltri_s[:], smallt[:, 5:6], start=True, stop=True),
          ["ltri", "tot2_"], ["pk7"], "sel_px2")
        R("dve", lambda e: e.tensor_copy(smallt[:, 6:7], ps_misc[:, 6:7]), ["pk7"], ["pfx2_", "pfx_"], "sel_pxe2")
        R("dve", lambda e: e.tensor_scalar(cs64[:], cs64[:], smallt[:, 6:7], None, ALU.add),
          ["cs2_", "pfx2_"], ["cs2_"], "sel_rk2")
        R("dve", lambda e: e.tensor_scalar(D16[:], cs64[:], rbase_s[:], -1.0, ALU.add, ALU.add),
          ["cs2_", "rbase"], ["D16"], "sel_d0")
        R("dve", lambda e: e.tensor_mul(D16[:], D16[:], L16[:]), ["D16", "SEL_"], ["D16"], "sel_d1")
        R("dve", lambda e: e.tensor_scalar(T16[:], L16[:], float(-SENT), float(SENT), ALU.mult, ALU.add),
          ["SEL_"], ["sent_", "sle_"], "sel_d2")
        R("dve", lambda e: e.tensor_add(D16[:], D16[:], T16[:]), ["D16", "sent_"], ["D16"], "sel_d3")
        R("dve", lambda e: e.tensor_copy(I16[:], D16[:]), ["D16"], ["I16"], "sel_i")
        B("sp", lambda: sch.dma(chS, "sp", qi_dr.rearrange("b (jj f) -> (b jj) f", f=64),
                  I16[0:64, :], reads=["I16"], writes=["qi_d"], name="st_qi"))
        R("dve", lambda e: e.tensor_scalar(D16[64:128, :], L16[64:128, :], 88.0, -88.0, ALU.mult, ALU.add),
          ["SEL_"], ["kvb", "D16"], "sel_kvb")
        B("sp", lambda: sch.dma(chS, "sp", kvb_dr.rearrange("b (jj f) -> (b jj) f", f=64),
                  D16[64:128, :], reads=["kvb"], writes=["kvb_d"], name="st_kvb"))
        if "rt_dbg" in dbg_names:
            rtdbg = scratch("rt_dbg", [128, 256], F32)
            def dmp(col, ap, rkey, nm):
                BG.append(lambda: sch.dma(chS, "sp", rtdbg[:, col:col + (ap.shape[-1] if hasattr(ap, 'shape') else 64)] if False else rtdbg[:, col:col + 64],
                          ap, reads=[rkey], writes=["rtd"], name=nm))
            def dmp1(col, ap, rkey, nm, wid=1):
                BG.append(lambda: sch.dma(chS, "sp", rtdbg[0:ap.shape[0] if False else 128, col:col + wid][0:128, :] if False else rtdbg[:, col:col + wid],
                          ap, reads=[rkey], writes=["rtd"], name=nm))
            B("sp", lambda: sch.dma(chS, "sp", rtdbg[:, 0:64], X16[:], reads=["X16"], writes=["rtd"], name="d_t"))
            B("sp", lambda: sch.dma(chS, "sp", rtdbg[:, 64:128], E16[:], reads=["tie2_"], writes=["rtd"], name="d_y"))
            B("sp", lambda: sch.dma(chS, "sp", rtdbg[:, 128:192], L16[:], reads=["SEL_"], writes=["rtd"], name="d_sel"))
            B("sp", lambda: sch.dma(chS, "sp", rtdbg[:, 192:256], cs64[:], reads=["cs2_"], writes=["rtd"], name="d_cs"))
            rtd2 = scratch("rt_dbg2", [128, 8], F32)
            def dsm(e):
                return e.dma_start(rtd2[:, 0:8], smallt[:, 0:8], allow_slow_non_contiguous=True)
            B("sp", lambda: sch.op("sp", dsm, reads=["need_", "apos", "cltpp_"], writes=["rtd2"],
                      name="d_sm", is_dma=True, dma_ch=chS))
            def dap(e):
                return e.dma_start(rtdbg[:, 252:254].rearrange("p (a o) -> p a o", a=2),
                                   _apcat(), allow_slow_non_contiguous=True)
            def _apcat():
                return apos[:]
            def dap2(e):
                return e.dma_start(rtdbg[:, 253:254], apos[:], allow_slow_non_contiguous=True)
            B("sp", lambda: sch.op("sp", dap2, reads=["apos"], writes=["rtd"], name="d_ap", is_dma=True, dma_ch=chS))
            def dlo(e):
                return e.dma_start(rtdbg[:, 254:255], smalli[:, 0:1].bitcast(F32), allow_slow_non_contiguous=True)
            B("sp", lambda: sch.op("sp", dlo, reads=["lo_"], writes=["rtd"], name="d_lo", is_dma=True, dma_ch=chS))

    nullq_bc = sb("nullq_bc", [128, D], BF16, side="right")
    iotac = sb("iotac", [128, 1], I32, side="right")

    def gen_gather():
        B("sp", lambda: sch.dma(chL, "sp", bqrow[0:1, 0:D], wd["nullq_row"],
                  writes=["nqrow"], name="ld_nq"))
        for hf in range(2):
            def fb(hf=hf):
                sch.op("pe", lambda e: e.matmul(ps_misc[:], ones1[:],
                       bqrow[0:1, hf * 512:(hf + 1) * 512], start=True, stop=True),
                       reads=["ones1", "nqrow"], writes=["pk7"], name=f"nq_bc{hf}")
            B("pe", fb)
            def fbe(hf=hf):
                sch.op("dve", lambda e: e.tensor_copy(nullq_bc[:, hf * 512:(hf + 1) * 512], ps_misc[:]),
                       reads=["pk7"], writes=["nullq_bc"], name=f"nq_ev{hf}")
            B("dve", fbe)
        for j in range(NT):
            def fpre(j=j):
                sch.dma(chS, "sp", ffnh_dr[j * 128:(j + 1) * 128, :], nullq_bc[:],
                        reads=["nullq_bc"], writes=[f"ffnh.{j}"], name=f"nqp{j}")
            B("sp", fpre)
        for sl0 in range(2):
            def fz(sl0=sl0):
                sch.op("dve", lambda e: e.memset(xgt[sl0][:, D + 2:D + 4], 0.0),
                       writes=[f"xgt{sl0}"], name=f"gz{sl0}")
            B("dve", fz)
        for j in range(NT):
            b, jt = j // 16, j % 16
            sl = j % 2
            def fx(j=j, sl=sl):
                sch.dma(chL, "sp", xgt[sl][:, 0:D], xnr_dr[j * 128:(j + 1) * 128, :],
                        reads=[f"xnrd.{j}"], writes=[f"xgt{sl}"], name=f"gx{j}")
            B("sp", fx)
            def fio(j=j, sl=sl):
                sch.op("pool", lambda e: e.iota(xgt[sl][:, D:D + 2].bitcast(I32), [[0, 1]],
                                               base=j * 128, channel_multiplier=1),
                       reads=[], writes=[f"xgt{sl}"], name=f"gio{j}")
            B("pool", fio)
            def fq(j=j, b=b, jt=jt, sl=sl):
                sch.dma(chL, "sp", qic[sl][:],
                        qi_dr[b, jt * 128:(jt + 1) * 128].rearrange("(p o) -> p o", o=1),
                        reads=["qi_d"], writes=[f"qic{sl}"], name=f"gqi{j}")
            B("sp", fq)
            def fsq(j=j, b=b, sl=sl):
                def op(e):
                    return e.indirect_dma_start(
                        xq_dr, IndirectOffsetOnAxis(qic[sl][:], 0),
                        xgt[sl][:], None,
                        bounds_check=BPC * NQ - 1, oob_is_err=False)
                sch.op("pool", op, reads=[f"xgt{sl}", f"qic{sl}"],
                       writes=[f"xqd.{j}"], name=f"gsq{j}", is_dma=True, dma_ch=chI)
            B("pool", fsq)


    def gen_heavy():
        light_keys = ["Wqkv_s", "lnb_b", "bqrow", "Bqk_pp", "Bv_bc", "ones1", "ones_col",
                      "qr0", "qr1", "prA0", "prA1", "prB0", "prB1", "avnL", "avlst", "rec8", "S16", "R16",
                      "X16", "Y16", "E16", "L16", "SEL_", "sle_", "tie_", "tie2_", "sent_",
                      "D16", "I16", "cs_", "cs2_", "O64", "pm1", "t4row", "m4", "S4sb",
                      "lsej", "t41", "ne4", "a4", "apos", "negm", "asg4", "asg4T", "ltri",
                      "logk4", "kpp", "rbase", "xgt0", "xgt1", "qic0", "qic1",
                      "nullq_bc", "nqrow", "kvb", "vones",
                      "pmax", "esum", "cnt_", "cpp_", "cge_", "cgei_", "ncge_",
                      "tmp_", "lo_", "hi_", "mid_", "cmp_", "need_", "tot_",
                      "pfx_", "tot2_", "pfx2_", "cltpp_"] + [f"kT.{t}" for t in range(4)] + \
                     [f"vt.{v}.full" for v in range(17)] + [f"vt.{v}.lo" for v in range(17)] + \
                     [f"vt.{v}.hi" for v in range(17)]
        def _emit_f2():
            sch.op("dve", lambda e: e.engine_nop(),
                   reads=[], writes=light_keys + ["fence2"], name="fence2")
            sch.set_fence("fence2")
        B("dve", _emit_f2)
        sb_free_back_to("Wblk1")
        WqT_s = sb("WqT_s", [128, 8 * 512], BF16, side="right")
        WkvT_s = sb("WkvT_s", [128, 8 * 1024], BF16, side="right")
        WouthT_s = sb("WouthT_s", [128, 4 * 1024], BF16, side="right")
        kTs = sb("kTs", [128, 4 * 2048], BF16, side="right")
        vds = sb("vds", [128, 16 * 520], BF16, side="right")
        qTs = sb("qTs", [128, 4 * 512], BF16, side="right")
        prpp = [sb(f"prpp{i}", [128, 512], BF16, side="right") for i in range(2)]
        avn4 = [sb(f"avn4_{i}", [128, 512], BF16, side="right") for i in range(4)]
        avhT = sb("avhT", [128, 512], BF16, side="right")
        hvst = [sb(f"hvst{i}", [128, D], BF16, side="right") for i in range(2)]
        nullk_s = sb("nullk_s", [128, 4], F32, side="right")
        nullk_b = sb("nullk_b", [128, 4], BF16, side="right")
        nv65f = sb("nv65f", [1, 8 * 65], F32, side="right")
        nv65b = sb("nv65b", [1, 8 * 65], BF16, side="right")
        nullpr = sb("nullpr", [1, 512], BF16, side="right")
        kvbc = sb("kvbc", [128, 16], F32, side="right")
        tki = [sb(f"tki{i}", [128, 1], I32, side="right") for i in range(2)]
        onec2 = sb("onec2", [128, 1], BF16, side="right")
        rcp1 = sb("rcp1", [128, 1], F32, side="right")
        xqTb = x2c[0]

        def HV(eng, fn, reads=(), writes=(), name=""):
            B(eng, lambda: sch.op(eng, fn, reads, writes, name))

        def wchunks(dst_s, src_name, inner, nch, dstkey):
            per = inner // 512
            for q in range(nch):
                t_, hf = q // per, q % per
                def f(q=q, t_=t_, hf=hf, src_name=src_name):
                    sch.dma(chL, "sp", stg[q % 2][:],
                            wd[src_name].rearrange("(t p) i -> p t i", p=128)[:, t_, hf * 512:(hf + 1) * 512],
                            reads=["fence2"], writes=[f"stg{q % 2}"], name=f"hw_{src_name}{q}")
                B("sp", f)
                def f2(q=q, dst_s=dst_s, dstkey=dstkey):
                    sch.op("dve", lambda e: e.tensor_copy(dst_s[:, q * 512:(q + 1) * 512], stg[q % 2][:]),
                           reads=[f"stg{q % 2}"], writes=[dstkey], name=f"hwc_{dstkey}{q}")
                B("dve", f2)
        wchunks(WqT_s, "WqT", 512, 8, "WqT_s")
        wchunks(WkvT_s, "WkvT", 1024, 16, "WkvT_s")
        wchunks(WouthT_s, "WouthT", 1024, 8, "WouthT_s")
        for dt in range(8):
            def f(dt=dt):
                sch.op("dve", lambda e: e.tensor_scalar(
                    WqT_s[:, dt * 512:(dt + 1) * 512], WqT_s[:, dt * 512:(dt + 1) * 512],
                    c_gamh[:, dt:dt + 1], None, ALU.mult),
                    reads=["WqT_s", "consts"], writes=["WqT_s"], name=f"hfq{dt}")
            B("dve", f)
        for dt in range(8):
            for hc in range(2):
                def f(dt=dt, hc=hc):
                    sch.op("dve", lambda e: e.tensor_scalar(
                        WkvT_s[:, dt * 1024 + hc * 512:dt * 1024 + (hc + 1) * 512],
                        WkvT_s[:, dt * 1024 + hc * 512:dt * 1024 + (hc + 1) * 512],
                        c_gamh[:, dt:dt + 1], None, ALU.mult),
                        reads=["WkvT_s", "consts"], writes=["WkvT_s"], name=f"hfk{dt}_{hc}")
                B("dve", f)
        B("sp", lambda: sch.dma(chL, "sp", nullk_s[:], wd["nullk_col"],
                  reads=["fence2"], writes=["nullk_s"], name="hnk"))
        B("sp", lambda: sch.dma(chL, "sp", nv65f[:], wd["nullv65"],
                  reads=["fence2"], writes=["nv65f"], name="hnv"))
        B("dve", lambda: sch.op("dve", lambda e: e.tensor_copy(nv65b[:], nv65f[:]),
                  reads=["nv65f"], writes=["nv65b"], name="hnvc"))
        B("dve", lambda: sch.op("dve", lambda e: e.tensor_copy(nullk_b[:], nullk_s[:]),
                  reads=["nullk_s"], writes=["nullk_b"], name="hnkc"))
        B("dve", lambda: sch.op("dve", lambda e: e.memset(onec2[:], 1.0),
                  reads=["fence2"], writes=["onec2"], name="honec"))
        B("dve", lambda: sch.op("dve", lambda e: e.memset(
                  vds[:].rearrange("p (s i) -> p s i", i=65)[:, :, 64:65], 1.0),
                  reads=["fence2"], writes=["vds"], name="hvone"))

        for b in range(BPC):
            for kt in range(16):
                def f(b=b, kt=kt):
                    sch.dma(chL, "sp", kvbc[:, kt:kt + 1],
                            kvb_dr[b, kt * 128:(kt + 1) * 128].rearrange("(p o) -> p o", o=1),
                            reads=["kvb_d"], writes=["kvbc"], name=f"hkb{b}_{kt}")
                B("sp", f)
            for ti in range(4):
                sl = ti % 2
                def f(b=b, ti=ti, sl=sl):
                    sch.dma(chL, "sp", hvst[sl][:], xq_dr[b * NQ + ti * 128:b * NQ + (ti + 1) * 128, 0:D],
                            reads=[f"xqd.{jj}" for jj in range(b * 16, (b + 1) * 16)],
                            writes=[f"hvst{sl}"], name=f"hxq{b}_{ti}")
                B("sp", f)
                for k in range(8):
                    def f2(b=b, ti=ti, sl=sl, k=k):
                        sch.op("pe", lambda e: e.transpose(
                            ps_bT[:, k * 128:(k + 1) * 128], hvst[sl][:, k * 128:(k + 1) * 128], ident_b[:]),
                            reads=[f"hvst{sl}", "identb"], writes=["pk4"], name=f"hxt{b}_{ti}_{k}")
                    B("pe", f2)
                def f3(b=b, ti=ti):
                    sch.op("dve", lambda e: e.tensor_copy(
                        xqTb[:].rearrange("p (dt t) -> p dt t", dt=8)[:, :, ti * 128:(ti + 1) * 128],
                        ps_bT[:].rearrange("p (dt t) -> p dt t", dt=8)),
                        reads=["pk4"], writes=["x2c0"], name=f"hxe{b}_{ti}")
                B("dve", f3)
            for it in range(4):
                bank = 2 + it % 2
                def f(b=b, it=it, bank=bank):
                    for dt in range(8):
                        sch.op("pe", lambda e, dt=dt: e.matmul(
                            ps_acc[bank][:],
                            WqT_s[:, dt * 512 + it * 128:dt * 512 + (it + 1) * 128],
                            xqTb[:, dt * 512:(dt + 1) * 512],
                            start=(dt == 0), stop=(dt == 7)),
                            reads=["WqT_s", "x2c0"], writes=[f"pk{bank}"], name=f"hq{b}_{it}_{dt}")
                B("pe", f)
                def f2(b=b, it=it, bank=bank):
                    sch.op("dve", lambda e: e.tensor_copy(qTs[:, it * 512:(it + 1) * 512], ps_acc[bank][:]),
                           reads=[f"pk{bank}"], writes=["qTs"], name=f"hqe{b}_{it}")
                B("dve", f2)
            for tc in range(4):
                csl = tc % 2
                for ti in range(4):
                    sl = ti % 2
                    jj = b * 16 + tc * 4 + ti
                    def f(sl=sl, jj=jj, b=b, tc=tc, ti=ti):
                        sch.dma(chL, "sp", hvst[sl][:], xnr_dr[jj * 128:(jj + 1) * 128, :],
                                reads=[f"xnrd.{jj}"], writes=[f"hvst{sl}"], name=f"hxn{b}_{tc}_{ti}")
                    B("sp", f)
                    for k in range(8):
                        def f2(sl=sl, k=k, b=b, tc=tc, ti=ti):
                            sch.op("pe", lambda e: e.transpose(
                                ps_bT[:, k * 128:(k + 1) * 128], hvst[sl][:, k * 128:(k + 1) * 128], ident_b[:]),
                                reads=[f"hvst{sl}", "identb"], writes=["pk4"], name=f"hnt{b}_{tc}_{ti}_{k}")
                        B("pe", f2)
                    def f3(csl=csl, ti=ti, b=b, tc=tc):
                        sch.op("dve", lambda e: e.tensor_copy(
                            x2c[csl][:].rearrange("p (dt t) -> p dt t", dt=8)[:, :, ti * 128:(ti + 1) * 128],
                            ps_bT[:].rearrange("p (dt t) -> p dt t", dt=8)),
                            reads=["pk4"], writes=[f"x2c{csl}"], name=f"hne{b}_{tc}_{ti}")
                    B("dve", f3)
                for p2 in range(4):
                    bank = 2 + p2 % 2
                    def f(csl=csl, p2=p2, b=b, tc=tc, bank=bank):
                        for h in (2 * p2, 2 * p2 + 1):
                            pb2 = (h % 2) * 64
                            for dt in range(8):
                                sch.op("pe", lambda e, h=h, dt=dt, pb2=pb2: e.matmul(
                                    ps_acc[bank][pb2:pb2 + 64, :],
                                    WkvT_s[:, dt * 1024 + h * 64:dt * 1024 + h * 64 + 64],
                                    x2c[csl][:, dt * 512:(dt + 1) * 512],
                                    start=(dt == 0), stop=(dt == 7)),
                                    reads=["WkvT_s", f"x2c{csl}"], writes=[f"pk{bank}"], name=f"hk{b}_{tc}_{h}_{dt}")
                    B("pe", f)
                    def f2(p2=p2, b=b, tc=tc, bank=bank):
                        sch.op("dve", lambda e: e.tensor_copy(
                            kTs[:, p2 * 2048 + tc * 512:p2 * 2048 + (tc + 1) * 512], ps_acc[bank][:]),
                            reads=[f"pk{bank}"], writes=["kTs"], name=f"hke{b}_{tc}_{p2}")
                    B("dve", f2)
                for m in range(4):
                    bank = 2 + m % 2
                    def f(csl=csl, m=m, b=b, tc=tc, bank=bank):
                        for dt in range(8):
                            sch.op("pe", lambda e, dt=dt: e.matmul(
                                ps_acc[bank][:],
                                x2c[csl][:, dt * 512 + m * 128:dt * 512 + (m + 1) * 128],
                                WkvT_s[:, dt * 1024 + 512:dt * 1024 + 1024],
                                start=(dt == 0), stop=(dt == 7)),
                                reads=["WkvT_s", f"x2c{csl}"], writes=[f"pk{bank}"], name=f"hv{b}_{tc}_{m}_{dt}")
                    B("pe", f)
                    def f2(m=m, b=b, tc=tc, bank=bank):
                        vt = tc * 4 + m
                        sch.op("dve", lambda e: e.tensor_copy(
                            vds[:].rearrange("p (kt h i) -> p kt h i", kt=16, h=8)[:, vt, :, 0:64],
                            ps_acc[bank][:].rearrange("p (h i) -> p h i", h=8)),
                            reads=[f"pk{bank}"], writes=["vds"], name=f"hve{b}_{tc}_{m}")
                    B("dve", f2)
            for h in range(8):
                pb, pr = (h % 2) * 64, h // 2
                def f(b=b, h=h, pb=pb, pr=pr):
                    sch.op("pe", lambda e: e.matmul(
                        ps_misc[0:1, :], nullk_b[pb:pb + 64, pr:pr + 1],
                        qTs[pb:pb + 64, (h // 2) * 512:(h // 2) * 512 + 512],
                        start=True, stop=True),
                        reads=["nullk_b", "qTs"], writes=["pk7"], name=f"hns{b}_{h}")
                B("pe", f)
                def f2(b=b, h=h):
                    sch.op("act", lambda e: e.activation(
                        nullpr[:], ps_misc[0:1, :], AF.Exp, scale=cim[0:1, 3:4]),
                        reads=["pk7", "cimm"], writes=["nullpr"], name=f"hnp{b}_{h}")
                B("act", f2)
                for pas in range(2):
                    tqa, tqb = 2 * pas, 2 * pas + 1
                    def f3(b=b, h=h, tqa=tqa, tqb=tqb):
                        for q_, bank in ((tqa, 2), (tqb, 3)):
                            sch.op("pe", lambda e, q_=q_, bank=bank: e.matmul(
                                ps_acc[bank][:, 0:65], nullpr[0:1, q_ * 128:(q_ + 1) * 128],
                                nv65b[0:1, h * 65:(h + 1) * 65],
                                start=True, stop=False),
                                reads=["nullpr", "nv65b"], writes=[f"pk{bank}"], name=f"hn1{b}_{h}_{q_}")
                    B("pe", f3)
                    prev_av = None
                    for kt in range(16):
                        psl = kt % 2
                        def f4(b=b, h=h, kt=kt, pb=pb, pr=pr, psl=psl, pas=pas):
                            sch.op("pe", lambda e: e.matmul(
                                ps_f[psl][:, 0:256],
                                kTs[pb:pb + 64, pr * 2048 + kt * 128:pr * 2048 + (kt + 1) * 128],
                                qTs[pb:pb + 64, (h // 2) * 512 + pas * 256:(h // 2) * 512 + (pas + 1) * 256],
                                start=True, stop=True),
                                reads=["kTs", "qTs"], writes=[f"pk{5 + psl}"], name=f"hsm{b}_{h}_{pas}_{kt}")
                        B("pe", f4)
                        if prev_av is not None:
                            B("pe", prev_av)
                        def f5(b=b, h=h, kt=kt, psl=psl, pas=pas):
                            sch.op("act", lambda e: e.activation(
                                prpp[psl][:, 0:256], ps_f[psl][:, 0:256], AF.Exp,
                                scale=cim[:, 3:4], bias=kvbc[:, kt:kt + 1]),
                                reads=[f"pk{5 + psl}", "cimm", "kvbc"], writes=[f"pr{psl}"], name=f"hex{b}_{h}_{pas}_{kt}")
                        B("act", f5)
                        def f6(b=b, h=h, kt=kt, psl=psl, tqa=tqa, tqb=tqb):
                            for ql, bank in ((0, 2), (1, 3)):
                                q_ = tqa + ql
                                sch.op("pe", lambda e, ql=ql, bank=bank, q_=q_: e.matmul(
                                    ps_acc[bank][:, 0:65], prpp[psl][:, ql * 128:(ql + 1) * 128],
                                    vds[:, kt * 520 + h * 65:kt * 520 + (h + 1) * 65],
                                    start=False, stop=(kt == 15)),
                                    reads=[f"pr{psl}", "vds"], writes=[f"pk{bank}"], name=f"ha{b}_{h}_{kt}_{q_}")
                        prev_av = f6
                    B("pe", prev_av)
                    for q_, bank in ((tqa, 2), (tqb, 3)):
                        def f8(b=b, h=h, q_=q_, bank=bank):
                            sch.op("dve", lambda e: e.reciprocal(rcp1[:], ps_acc[bank][:, 64:65]),
                                   reads=[f"pk{bank}"], writes=["rcp1"], name=f"hrc{b}_{h}_{q_}")
                            sch.op("dve", lambda e: e.tensor_scalar(
                                avn4[q_][:, h * 64:(h + 1) * 64], ps_acc[bank][:, 0:64],
                                rcp1[:], None, ALU.mult),
                                reads=[f"pk{bank}", "rcp1"], writes=[f"avn{q_}"], name=f"hnm{b}_{h}_{q_}")
                        B("dve", f8)
            for tqt in range(4):
                for ip in range(4):
                    def f(b=b, tqt=tqt, ip=ip):
                        sch.op("pe", lambda e: e.transpose(
                            ps_bT[:, ip * 128:(ip + 1) * 128], avn4[tqt][:, ip * 128:(ip + 1) * 128], ident_b[:]),
                            reads=[f"avn{tqt}", "identb"], writes=["pk4"], name=f"hot{b}_{tqt}_{ip}")
                    B("pe", f)
                def f2(b=b, tqt=tqt):
                    sch.op("dve", lambda e: e.tensor_copy(avhT[:], ps_bT[:, 0:512]),
                           reads=["pk4"], writes=["avhT"], name=f"hoe{b}_{tqt}")
                B("dve", f2)
                sl = tqt % 2
                for cc in range(2):
                    bank = 2 + cc
                    def f3(b=b, tqt=tqt, cc=cc, bank=bank):
                        for ip in range(4):
                            sch.op("pe", lambda e, ip=ip: e.matmul(
                                ps_acc[bank][:], avhT[:, ip * 128:(ip + 1) * 128],
                                WouthT_s[:, ip * 1024 + cc * 512:ip * 1024 + (cc + 1) * 512],
                                start=(ip == 0), stop=(ip == 3)),
                                reads=["avhT", "WouthT_s"], writes=[f"pk{bank}"], name=f"how{b}_{tqt}_{cc}_{ip}")
                    B("pe", f3)
                    def f4(b=b, tqt=tqt, cc=cc, sl=sl, bank=bank):
                        sch.op("dve", lambda e: e.tensor_copy(
                            hvst[sl][:, cc * 512:(cc + 1) * 512], ps_acc[bank][:]),
                            reads=[f"pk{bank}"], writes=[f"hvst{sl}"], name=f"hoc{b}_{tqt}_{cc}")
                    B("dve", f4)
                def f5(b=b, tqt=tqt, sl=sl):
                    def op(e):
                        return e.dma_start(tki[sl][:],
                            xq_dr[b * NQ + tqt * 128:b * NQ + (tqt + 1) * 128, D:D + 2].bitcast(I32),
                            allow_slow_non_contiguous=True)
                    sch.op("sp", op, reads=[f"xqd.{jj}" for jj in range(b * 16, (b + 1) * 16)],
                           writes=[f"tki{sl}"], name=f"hti{b}_{tqt}", is_dma=True, dma_ch=chL)
                B("sp", f5)
                def f6(b=b, tqt=tqt, sl=sl):
                    def op(e):
                        return e.indirect_dma_start(
                            ffnh_dr, IndirectOffsetOnAxis(tki[sl][:], 0),
                            hvst[sl][:], None,
                            bounds_check=TOK - 1, oob_is_err=False)
                    sch.op("pool", op, reads=[f"hvst{sl}", f"tki{sl}"] + [f"ffnh.{j}" for j in range(NT)],
                           writes=[f"hvs.{b}.{tqt}"], name=f"hsc{b}_{tqt}", is_dma=True, dma_ch=chI)
                B("pool", f6)



    if 'light' in _PH:
        gen_light()
    if 'route' in _PH:
        gen_route()
    if 'gather' in _PH:
        gen_gather()
    if 'heavy' in _PH:
        gen_heavy()

    def _emit_fence3():
        td = sorted(sch.last_w)
        sch.op("dve", lambda e: e.engine_nop(),
               reads=[], writes=td + ["fence3"], name="fence3")
        sch.set_fence("fence3")
    pend_bg.append(("dve", _emit_fence3))

    sb_free_back_to("__scan__")
    xti = [sb(f"xti{i}", [128, D], F32, side="right") for i in range(2)]
    reci = [sb(f"reci{i}", [128, D], BF16, side="right") for i in range(2)]
    fhi = [sb(f"fhi{i}", [128, D], BF16, side="right") for i in range(2)]
    avli = [sb(f"avli{i}", [128, 4 * 128], BF16, side="right") for i in range(2)]
    WoutlT_s = sb("WoutlT_s", [128, 4 * 1024], BF16, side="right")
    wstg = sb("wstg", [128, 2048], F32, side="right")
    Bout_bc = sb("Bout_bc", [128, D], F32, side="right")
    brow = sb("brow", [1, D], F32, side="right")
    one1t = sb("one1t", [1, 128], F32, side="right")
    ffnt = sb("ffnt", [128, D], F32, side="right")
    silt = sb("silt", [128, D], F32, side="right")
    st_ = sb("st_", [128, D], F32, side="right")
    yst = [sb(f"yst{i}", [128, D], F32, side="right") for i in range(2)]

    tail_hold = {jt: [] for jt in range(16)}
    _tail_cur = [None]

    def T(tag, fn):
        if _tail_cur[0] is None:
            pend_tail.append((tag, fn))
        else:
            tail_hold[_tail_cur[0]].append((tag, fn))

    def gen_tail():
        for ch in range(2):
            T("sp", lambda ch=ch: sch.dma(chL, "sp", wstg[:].rearrange("p (a c) -> p a c", a=2),
                    wd["WoutlT"].rearrange("(it p) c -> p it c", p=128)[:, 2 * ch:2 * ch + 2, :],
                    reads=["fence3"], writes=["wstg"], name=f"wl{ch}"))
            T("dve", lambda ch=ch: sch.op("dve", lambda e, c=ch: e.tensor_copy(WoutlT_s[:, c * 2048:(c + 1) * 2048], wstg[:]),
                   reads=["wstg"], writes=["WoutlT_s"], name=f"wlc{ch}"))
        T("sp", lambda: sch.dma(chL, "sp", brow[:], wd["bout_row"], reads=["fence3"], writes=["brow"], name="ld_brow"))
        T("dve", lambda: sch.op("dve", lambda e: e.memset(one1t[:], 1.0), reads=["fence3"], writes=["one1t"], name="one1t"))
        T("pe", lambda: sch.op("pe", lambda e: e.matmul(ps_misc[:], one1t[:], brow[0:1, 0:512], start=True, stop=True),
               reads=["one1t", "brow"], writes=["pk7"], name="bb0"))
        T("dve", lambda: sch.op("dve", lambda e: e.tensor_copy(Bout_bc[:, 0:512], ps_misc[:]),
               reads=["pk7"], writes=["Bout_bc"], name="bb0e"))
        T("pe", lambda: sch.op("pe", lambda e: e.matmul(ps_misc[:], one1t[:], brow[0:1, 512:1024], start=True, stop=True),
               reads=["one1t", "brow"], writes=["pk7"], name="bb1"))
        T("dve", lambda: sch.op("dve", lambda e: e.tensor_copy(Bout_bc[:, 512:1024], ps_misc[:]),
               reads=["pk7"], writes=["Bout_bc"], name="bb1e"))

        for jt in range(16):
          for b in range(BPC):
            j = b * 16 + jt
            sl = b
            _tail_cur[0] = jt
            T("sp", lambda j=j, sl=sl: sch.dma(chL, "sp", xti[sl][:], x_d[j * 128:(j + 1) * 128, :],
                    writes=[f"xti{sl}"], name=f"tx{j}"))
            T("sp", lambda j=j, sl=sl, b=b, jt=jt: sch.dma(chU, "sp", reci[sl][:], rec_dr[j * 128:(j + 1) * 128, :],
                    reads=[f"recd.{jt}.{b}.{c}" for c in range(2)],
                    writes=[f"reci{sl}"], name=f"tr{j}"))
            T("sp", lambda j=j, sl=sl: sch.dma(chU, "sp", fhi[sl][:], ffnh_dr[j * 128:(j + 1) * 128, :],
                    reads=[f"ffnh.{j}"] + [f"hvs.{bb}.{q}" for bb in range(BPC) for q in range(4)],
                    writes=[f"fhi{sl}"], name=f"tf{j}"))
            T("sp", lambda j=j, sl=sl, b=b, jt=jt: sch.dma(chU, "sp", avli[sl][:],
                    avlT_dr.rearrange("p (ip t) -> p ip t", ip=4)[:, :, b * 2048 + jt * 128:b * 2048 + (jt + 1) * 128],
                    reads=[f"avld.{b}.{jt}"], writes=[f"avli{sl}"], name=f"ta{j}"))
            for cc in range(2):
                def tmm(j=j, cc=cc, sl=sl):
                    for ip in range(4):
                        sch.op("pe", lambda e, c=cc, ip=ip, ss=sl: e.matmul(
                            ps_acc[c][:], avli[ss][:, ip * 128:(ip + 1) * 128],
                            WoutlT_s[:, ip * 1024 + c * 512:ip * 1024 + (c + 1) * 512],
                            start=(ip == 0), stop=(ip == 3)),
                            reads=[f"avli{sl}", "WoutlT_s"], writes=[f"pk{cc}"], name=f"tl{j}_{cc}_{ip}")
                T("pe", tmm)
            for cc in range(2):
                T("dve", lambda j=j, cc=cc, sl=sl: sch.op("dve", lambda e, c=cc, ss=sl: e.tensor_tensor(
                    ffnt[:, c * 512:(c + 1) * 512], ps_acc[c][:], fhi[ss][:, c * 512:(c + 1) * 512], ALU.add),
                    reads=[f"pk{cc}", f"fhi{sl}"], writes=["ffnt"], name=f"tff{j}_{cc}"))
            for hc in range(2):
                T("act", lambda j=j, hc=hc: sch.op("act", lambda e: e.activation(
                       silt[:, hc * 512:(hc + 1) * 512], ffnt[:, hc * 512:(hc + 1) * 512],
                       AF.Tanh, scale=cim[:, CI_H:CI_H + 1]),
                       reads=["ffnt", "cimm"], writes=["silt"], name=f"tsil{j}_{hc}"))
            for hc in range(2):
                cs = slice(hc * 512, (hc + 1) * 512)
                T("dve", lambda j=j, sl=sl, cs=cs, hc=hc: sch.op("dve", lambda e, ss=sl, jj=j: e.tensor_scalar(
                    st_[:, cs], xti[ss][:, cs], stats[:, NT + jj:NT + jj + 1], stats[:, jj:jj + 1], ALU.mult, ALU.add),
                    reads=[f"xti{sl}", f"r1.{j}", f"nb1.{j}"], writes=["st_"], name=f"txn{j}_{hc}"))
                T("dve", lambda j=j, sl=sl, cs=cs, hc=hc: sch.op("dve", lambda e, ss=sl: e.tensor_tensor(
                    st_[:, cs], st_[:, cs], reci[ss][:, cs], ALU.add),
                    reads=["st_", f"reci{sl}"], writes=["st_"], name=f"ts1{j}_{hc}"))
                T("dve", lambda j=j, cs=cs, hc=hc: sch.op("dve", lambda e: e.tensor_tensor(
                    st_[:, cs], st_[:, cs], Bout_bc[:, cs], ALU.add),
                    reads=["st_", "Bout_bc"], writes=["st_"], name=f"ts2{j}_{hc}"))
                T("dve", lambda j=j, sl=sl, cs=cs, hc=hc: sch.op("dve", lambda e, ss=sl: e.tensor_mul(
                    yst[ss][:, cs], st_[:, cs], ffnt[:, cs]),
                    reads=["st_", "ffnt"], writes=[f"yst{sl}"], name=f"ty1{j}_{hc}"))
                T("dve", lambda j=j, sl=sl, cs=cs, hc=hc: sch.op("dve", lambda e, ss=sl: e.scalar_tensor_tensor(
                    st_[:, cs], yst[ss][:, cs], 0.5, xti[ss][:, cs], ALU.mult, ALU.add),
                    reads=[f"yst{sl}", f"xti{sl}", "st_"], writes=["st_"], name=f"ty2{j}_{hc}"))
                T("dve", lambda j=j, sl=sl, cs=cs, hc=hc: sch.op("dve", lambda e, ss=sl: e.tensor_mul(
                    yst[ss][:, cs], yst[ss][:, cs], silt[:, cs]),
                    reads=[f"yst{sl}", "silt"], writes=[f"yst{sl}"], name=f"ty3{j}_{hc}"))
                T("dve", lambda j=j, sl=sl, cs=cs, hc=hc: sch.op("dve", lambda e, ss=sl: e.scalar_tensor_tensor(
                    yst[ss][:, cs], yst[ss][:, cs], 0.5, st_[:, cs], ALU.mult, ALU.add),
                    reads=[f"yst{sl}", "st_"], writes=[f"yst{sl}"], name=f"ty4{j}_{hc}"))
            T("sp", lambda j=j, sl=sl: sch.dma(chS, "sp", y_d[j * 128:(j + 1) * 128, :], yst[sl][:],
                    reads=[f"yst{sl}"], writes=[f"ydone.{j}"], name=f"ty{j}"))
        _tail_cur[0] = None

    gen_tail()

    emit_pre(0)
    while pend:
        inject(1)
    sch.op("dve", lambda e: e.tensor_copy(argb[0][:], col16(Uring[0], 0)),
           reads=["Ur0"], writes=["arg0"], name="arginit")

    for ib in range(NBLK):
        r = ib % 2
        if ib + 1 < NBLK:
            emit_pre_dma(ib + 1)
        if ib > 0:
            pass
        for tloc in range(BS):
            if tloc == 40 and ib + 1 < NBLK:
                emit_pre_comp(ib + 1)
            t = ib * BS + tloc
            pa = t % 2
            inject_step()
            ka, kan = f"arg{pa}", f"arg{1 - pa}"
            km, kp, kp2 = f"m{t % 2}", f"p{t % 2}", f"p2{t % 2}"
            last = (t == L - 1)
            if not last:
                sch.op("pool", lambda e, r=r, tl=tloc, pa=pa: e.tensor_tensor(
                    pb[pa][:], col16(gblk[r], tl), argb[pa][:], ALU.mult),
                    reads=[f"gb{r}", ka], writes=[kp], name=f"p{t}")
            sch.op("act", lambda e, r=r, tl=tloc, pa=pa: e.activation(
                col16(Wblk[r], tl), argb[pa][:], AF.Tanh),
                reads=[ka], writes=[f"Wb{r}.f"], name=f"w{t}")
            if not last:
                sch.op("pool", lambda e, r=r, tl=tloc, pa=pa: e.tensor_tensor(
                    p2b[pa][:], pb[pa][:], col16(kblk[r], tl), ALU.add),
                    reads=[kp, f"kb{r}"], writes=[kp2], name=f"p2{t}")
                sch.op("pool", lambda e, r=r, tl=tloc, pa=pa: e.tensor_tensor(
                    mb[pa][:], col16(ACblk[r], tl), col16(Wblk[r], tl), ALU.mult),
                    reads=[f"ACb{r}", f"Wb{r}.f"], writes=[km], name=f"m{t}")
                sch.op("pool", lambda e, pa=pa: e.tensor_tensor(
                    argb[1 - pa][:], mb[pa][:], p2b[pa][:], ALU.add),
                    reads=[km, kp2], writes=[kan], name=f"an{t}")
        emit_post(ib)
        pend_tail.extend(tail_hold.pop(ib, ()))

    while pend or pend_low or pend_bg or pend_tail:
        inject(1)


    fin_reads = [f"ydone.{j}" for j in range(NT)]
    fin_reads += [k for k in list(sch.last_w) if k.startswith("dbgdone")]
    sch.op("dve", lambda e: e.engine_nop(), reads=fin_reads, writes=[], name="fin")
    sch.finalize_and_emit(es)
    return nc, es


def _pack_inputs(inputs, core):
    inp = {}
    x = np.asarray(inputs["L0_attn_x"])
    inp["x"] = _r(x[core * BPC:(core + 1) * BPC].reshape(TOK, D))
    inp["W_in"] = _r(np.asarray(inputs["L0_attn_W_in"]))
    inp["WqkvT"] = _r(np.asarray(inputs["L0_ffn_Wqkv"]).T)
    inp["WqT"] = _r(np.asarray(inputs["L0_ffn_Wq"]).T)
    _wkvT = np.asarray(inputs["L0_ffn_Wkv"]).T
    _kcols = (np.arange(8)[:, None] * 128 + np.arange(64)[None, :]).reshape(-1)
    _vcols = _kcols + 64
    inp["WkvT"] = _r(_wkvT[:, np.concatenate([_kcols, _vcols])])
    inp["WoutlT"] = _r(np.asarray(inputs["L0_ffn_Wout_l"]).T)
    inp["WouthT"] = _r(np.asarray(inputs["L0_ffn_Wout_h"]).T)
    inp["W_out"] = _r(np.asarray(inputs["L0_attn_W_out"]))
    rs = lambda v: _r(np.asarray(v).reshape(8, 128).T)
    inp["gamma_r"] = rs(inputs["L0_attn_gamma"])
    inp["beta_r"] = rs(inputs["L0_attn_beta"])
    inp["A_r"] = rs(inputs["L0_attn_A"])
    inp["b_in_r"] = _r(np.asarray(inputs["L0_attn_b_in"]).reshape(16, 128).T)
    inp["b_out_r"] = rs(inputs["L0_attn_b_out"])
    inp["ln_w_r"] = rs(inputs["L0_ffn_ln_w"])
    inp["ln_b_r"] = rs(inputs["L0_ffn_ln_b"])
    inp["gamh_r"] = rs(inputs["L0_ffn_gamma"])
    rt2 = np.stack([np.asarray(inputs["L0_ffn_rt_q"]).reshape(D),
                    np.asarray(inputs["L0_ffn_rt_kv"]).reshape(D)], axis=1)
    inp["rt2_r"] = _r(rt2.reshape(8, 128, 2).transpose(1, 0, 2).reshape(128, 16))
    nk = np.asarray(inputs["L0_ffn_null_kv"])
    inp["nullk_row"] = _r(nk[0].reshape(1, H * DH))
    inp["nullv_row"] = _r(nk[1].reshape(1, H * DH))
    inp["nullq_row"] = _r(np.asarray(inputs["L0_ffn_null_q"]).reshape(1, D))
    import ml_dtypes
    inp["ident_bf"] = _r(np.eye(128, dtype=ml_dtypes.bfloat16))
    inp["ident_f32"] = _r(np.eye(128, dtype=np.float32))
    cimm = np.zeros((128, 16), np.float32)
    cimm[:, 0] = LN_EPS
    cimm[:, 1] = -0.5
    cimm[:, 2] = 0.5
    cimm[:, 3] = 0.125
    cimm[:, 4] = float(np.log(2048.0))
    cimm[:, 5] = -1.0
    cimm[:, 6] = 1.0
    inp["cimm"] = cimm
    asg = np.zeros((4, 128), np.float32)
    for r4 in range(4):
        asg[r4, r4 * 32:(r4 + 1) * 32] = 1.0
    inp["asg4"] = _r(asg)
    inp["asg4T"] = _r(asg.T)
    lt = np.zeros((128, 128), np.float32)
    for g4 in range(4):
        for a_ in range(32):
            for b_ in range(32):
                if a_ < b_:
                    lt[g4 * 32 + a_, g4 * 32 + b_] = 1.0
    inp["ltri"] = lt
    inp["logk4"] = _r(np.array([[LOGK_Q], [LOGK_Q], [LOGK_KV], [LOGK_KV]], np.float32))
    kp = np.zeros((128, 1), np.float32)
    kp[0:64] = float(NQ)
    kp[64:128] = float(NKV)
    inp["kpp"] = kp
    rb = np.zeros((128, 1), np.float32)
    rb[32:64] = float(NQ)
    rb[96:128] = float(NKV)
    inp["rbase"] = rb
    nk4 = np.asarray(inputs["L0_ffn_null_kv"])[0].reshape(512)
    inp["nullk_col"] = _r(nk4.reshape(4, 128).T)
    nv65 = np.zeros((1, 8 * 65), np.float32)
    nv = np.asarray(inputs["L0_ffn_null_kv"])[1].reshape(8, 64)
    for h_ in range(8):
        nv65[0, h_ * 65:h_ * 65 + 64] = nv[h_]
        nv65[0, h_ * 65 + 64] = 1.0
    inp["nullv65"] = nv65
    inp["bout_row"] = _r(np.asarray(inputs["L0_attn_b_out"]).reshape(1, D))
    return inp


_CACHE = {}


def kernel(debug=(), **inputs):
    key = tuple(sorted(debug))
    if key not in _CACHE:
        _CACHE[key] = build_program(debug)
    nc, es = _CACHE[key]
    in_maps = [_pack_inputs(inputs, c) for c in range(NCORES)]
    res = run_bass_kernel_spmd(nc, in_maps, core_ids=list(range(NCORES)))
    y = np.stack([r["y"] for r in res.results], axis=0).reshape(B, L, D)
    if debug:
        dbgs = [dict(r) for r in res.results]
        return y, dbgs
    return y
